# version 05
# baseline (speedup 1.0000x reference)
import numpy as np
import ml_dtypes
import concourse.bass as bass
import concourse.mybir as mybir
from concourse.bass_utils import run_bass_kernel_spmd

F32 = mybir.dt.float32
BF16 = mybir.dt.bfloat16
I32 = mybir.dt.int32
AF = mybir.ActivationFunctionType
ALU = mybir.AluOpType
AX = mybir.AxisListType

D = 2048
T = 2048
TQ = 1024
HID = 5632
KC = 16
HG = 2
NHG = 8
SCL = 0.08838834764831845
W_N = float(HID * D)
P1T = 1024
NTM = 88
KB = 1024
DEBUG_DUMP = False
STOP_AFTER = None

ENG_SEM = {"tensor": "mm", "vector": "vec", "scalar": "act"}


class Sched:
    def __init__(self, nc):
        self.nc = nc
        self.items = {e: [] for e in ("sync", "tensor", "vector", "scalar", "gpsimd")}
        self.sems = {}
        self.counts = {}
        for s in ("mm", "vec", "act"):
            self.sems[s] = nc.alloc_semaphore(s)
            self.counts[s] = 0
        self.nwaits = 0

    def get_sem(self, name):
        if name not in self.sems:
            self.sems[name] = self.nc.alloc_semaphore(name)
            self.counts[name] = 0
        return self.sems[name]

    def emit(self, eng, fn):
        self.items[eng].append(fn)

    def inc(self, name, n=1):
        self.get_sem(name)
        self.counts[name] += n
        return self.sems[name], self.counts[name]

    def val(self, name):
        return self.counts[name]

    def wait(self, eng, name, v):
        if v <= 0:
            return
        if ENG_SEM.get(eng) == name:
            return
        self.nwaits += 1
        s = self.get_sem(name)
        self.emit(eng, lambda E, s=s, v=v: E.wait_ge(s, v))


class Res:

    def __init__(self, S):
        self.S = S
        self.w = None
        self.r = {}

    def write(self, eng):
        if self.w:
            self.S.wait(eng, *self.w)
        for sn, v in self.r.items():
            self.S.wait(eng, sn, v)

    def wrote(self, semname, val):
        self.w = (semname, val)
        self.r = {}

    def read(self, eng):
        if self.w:
            self.S.wait(eng, *self.w)

    def did_read(self, semname, val):
        self.r[semname] = max(self.r.get(semname, 0), val)


def build_program():
    nc = bass.Bass("TRN2", target_bir_lowering=False, detect_race_conditions=False)

    din = lambda n, s, dt=F32: nc.dram_tensor(n, s, dt, kind="ExternalInput").ap()
    x_kvb = din("x_kvb", [8, 128, KC * 256])
    x_qb = din("x_qb", [4, 128, KC * 256])
    x_q_tm = din("x_q_tm", [TQ, D])
    Wqb2 = din("Wqb2", [KC, 128, D])
    Wkb = din("Wkb", [NHG, 128, KC * 256])
    Wvb = din("Wvb", [NHG, 128, KC * 256])
    Wod = din("Wo", [D, D])
    bqd, bkd = din("bq", [128, 16]), din("bk", [128, 16])
    bvd, bod = din("bv", [1, D]), din("bo", [1, D])
    w1bd = din("w1b", [11, 128, KC * 512])
    w2bd = din("w2b", [11, 128, KC * 512])
    w3bd = din("w3b", [KC, 128, (HID // 128) * 128])
    maskd = din("mask", [TQ, T], BF16)
    eyebd = din("eyeb", [128, 128], BF16)
    eyefd = din("eyef", [128, 128], F32)
    eyepd = din("eyep", [128, 128], F32)
    y_out = nc.dram_tensor("y", [TQ, D], F32, kind="ExternalOutput").ap()

    _scrkind = dict(kind="ExternalOutput") if DEBUG_DUMP else {}
    out_scr = nc.dram_tensor("out_scr", [D, TQ], F32, **_scrkind).ap()
    q_scr = nc.dram_tensor("q_scr", [16, 128, 1024], F32).ap()
    x1_scr = nc.dram_tensor("x1_scr", [TQ, D], F32, **_scrkind).ap()
    t1_scr = nc.dram_tensor("t1_scr", [D * HID], BF16, **_scrkind).ap()
    t2_scr = nc.dram_tensor("t2_scr", [D * HID], BF16, **_scrkind).ap()
    t3_scr = nc.dram_tensor("t3_scr", [KC * HID * 128], BF16, **_scrkind).ap()
    h_dbg = nc.dram_tensor("h_dbg", [128, 2 * HID], F32, kind="ExternalOutput").ap() if DEBUG_DUMP else None
    r2_dbg = nc.dram_tensor("r2_dbg", [TQ, D], F32, kind="ExternalOutput").ap() if DEBUG_DUMP else None
    hq_dbg = nc.dram_tensor("hq_dbg", [128, (HID // 128) * 512], BF16, kind="ExternalOutput").ap() if DEBUG_DUMP else None
    ha_dbg = nc.dram_tensor("ha_dbg", [128, 8], F32, kind="ExternalOutput").ap() if DEBUG_DUMP else None
    h2_dbg = nc.dram_tensor("h2_dbg", [128, 2 * HID], F32, kind="ExternalOutput").ap() if DEBUG_DUMP else None
    xq_dbg = nc.dram_tensor("xq_dbg", [128, KC * TQ], BF16, kind="ExternalOutput").ap() if DEBUG_DUMP else None
    ln2_dbg = nc.dram_tensor("ln2_dbg", [128, 8 * 16], F32, kind="ExternalOutput").ap() if DEBUG_DUMP else None
    q_dbg = nc.dram_tensor("q_dbg", [128, 4 * 512], F32, kind="ExternalOutput").ap() if DEBUG_DUMP else None
    r1_dbg = nc.dram_tensor("r1_dbg", [TQ, D], F32, kind="ExternalOutput").ap() if DEBUG_DUMP else None
    bo_dbg = nc.dram_tensor("bo_dbg", [128, D], F32, kind="ExternalOutput").ap() if DEBUG_DUMP else None
    ln_dbg = nc.dram_tensor("ln_dbg", [128, 8 * 32], F32, kind="ExternalOutput").ap() if DEBUG_DUMP else None

    sb = nc.alloc_sbuf_tensor
    consts = sb("consts", [128, 8], F32)
    ones_row = sb("ones_row", [1, 128], F32)
    eye_b = sb("eye_b", [128, 128], BF16)
    eye_f = sb("eye_f", [128, 128], F32)
    eye_p = sb("eye_p", [128, 128], F32)
    bqc = sb("bqc", [128, 16], F32)
    bkc = sb("bkc", [128, 16], F32)
    bv_row = sb("bv_row", [1, 256], F32)
    bo_row = sb("bo_row", [1, 512], F32)
    bvbc = sb("bvbc", [128, 256], F32)
    p1a = sb("p1a", [128, 3 * NTM], F32)
    wmean = sb("wmean", [128, 16], F32)
    stt_ = sb("stt_", [128, 16], F32)
    ws = sb("ws", [128, 2 * P1T], F32)
    tt_b = sb("tt_b", [128, P1T], BF16)
    ta_b = sb("ta_b", [128, P1T], BF16)
    qy = sb("qy", [128, 512], F32)
    qi = sb("qi", [128, 512], I32)
    qd = sb("qd", [128, 512], F32)
    qa = sb("qa", [128, 512], F32)
    qb = sb("qb", [128, 512], F32)
    qk = sb("qk", [128, 512], BF16)
    sxa = sb("sxa", [128, 8], F32)
    dq1 = sb("dq1", [128, 8], F32)
    dq2 = sb("dq2", [128, 8], F32)
    dq3 = sb("dq3", [128, 8], F32)
    hamax = sb("hamax", [128, 8], F32)
    lnst = sb("lnst", [128, 16], F32)
    spc = sb("spc", [128, 512], F32)

    base = int(nc.bytes_allocated(bass.MemorySpace.SBUF))
    base = (base + 127) & ~127

    def at(name, shape, dt, off):
        return nc.alloc_sbuf_tensor_at(name, shape, dt, offset=base + off)

    xs = at("xs", [128, 2 * KC * 256], F32, 0)
    k_fm = at("k_fm", [128, HG * T], F32, 32 * KB)
    v_tm = at("v_tm", [128, 16 * HG * 128], F32, 48 * KB)
    q_h = at("q_h", [128, 2 * 1024], F32, 64 * KB)
    wq_s = at("wq_s", [128, KC * 256], F32, 72 * KB)
    wk_s = at("wk_s", [128, KC * 256], F32, 88 * KB)
    wv_s = at("wv_s", [128, KC * 256], F32, 104 * KB)
    e_sb = at("e_sb", [128, 4 * T], F32, 120 * KB)
    eT_sb = at("eT_sb", [128, 2 * 256], F32, 152 * KB)
    msk_s = at("msk_s", [128, 2 * T], BF16, 154 * KB)
    o_sb = at("o_sb", [128, 2 * 256], F32, 162 * KB)
    qgb = at("qgb", [128, 2 * 128], F32, 164 * KB)

    r1 = at("r1", [128, 8 * D], F32, 0)
    wo_s = at("wo_s", [128, 2 * 512], F32, 64 * KB)
    os_s = at("os_s", [128, 2 * 512], F32, 68 * KB)
    o3_sb = at("o3_sb", [128, 512], F32, 72 * KB)
    xqs = at("xqs", [128, 4 * D], F32, 74 * KB)
    bo_bc = at("bo_bc", [128, D], F32, 106 * KB)
    sq_scr = at("sq_scr", [128, D], F32, 114 * KB)
    x1_sb = at("x1_sb", [128, D], F32, 122 * KB)
    xq_fm = at("xq_fm", [128, KC * TQ], BF16, 132 * KB)

    h_q = at("h_q", [128, 2 * HID], F32, 0)
    t_s = at("t_s", [128, 2 * KC * 512], BF16, 44 * KB)
    hq_fm = at("hq_fm", [128, (HID // 128) * 512], BF16, 76 * KB)
    hs1 = at("hs1", [128, 2 * 512], F32, 120 * KB)
    hs2 = at("hs2", [128, 2 * 512], F32, 124 * KB)
    t3_sb = at("t3_sb", [128, 2 * HID], BF16, 0)
    r2 = at("r2", [128, 4 * D], F32, 22 * KB)
    fr_sb = at("fr_sb", [128, 1024], F32, 54 * KB)
    y_sb = at("y_sb", [128, D], F32, 58 * KB)

    ps_big_t = nc.alloc_psum_tensor("ps_big", [128, 2048], F32)
    ps_mid_t = nc.alloc_psum_tensor("ps_mid", [128, 1024], F32)
    ps_a_t = nc.alloc_psum_tensor("ps_a", [128, 512], F32)
    ps_b_t = nc.alloc_psum_tensor("ps_b", [128, 512], F32)

    S = Sched(nc)

    def dma(eng, sem_name, out_ap, in_ap):
        S.get_sem(sem_name)
        s, v = S.inc(sem_name, 16)
        S.emit(eng, lambda E, o=out_ap, i=in_ap, s=s: E.dma_start(out=o, in_=i).then_inc(s, 16))
        return v

    def mmi(out_ap, lh, rh, start, stop, inc=False):
        if inc:
            s, v = S.inc("mm")
            S.emit("tensor", lambda E, o=out_ap, l=lh, r=rh, a=start, b=stop, s=s:
                   E.matmul(o, l, r, start=a, stop=b).then_inc(s, 1))
            return v
        S.emit("tensor", lambda E, o=out_ap, l=lh, r=rh, a=start, b=stop:
               E.matmul(o, l, r, start=a, stop=b))
        return None

    def tri(out_ap, in_ap, ident, inc=False):
        if inc:
            s, v = S.inc("mm")
            S.emit("tensor", lambda E, o=out_ap, i=in_ap, d=ident, s=s:
                   E.transpose(o, i, d).then_inc(s, 1))
            return v
        S.emit("tensor", lambda E, o=out_ap, i=in_ap, d=ident: E.transpose(o, i, d))
        return None

    def vop(fn, inc=True):
        if inc:
            s, v = S.inc("vec")
            S.emit("vector", lambda E, f=fn, s=s: f(E).then_inc(s, 1))
            return v
        S.emit("vector", lambda E, f=fn: f(E))
        return None

    def aop(fn, inc=True):
        if inc:
            s, v = S.inc("act")
            S.emit("scalar", lambda E, f=fn, s=s: f(E).then_inc(s, 1))
            return v
        S.emit("scalar", lambda E, f=fn: f(E))
        return None

    def vdrain():
        S.emit("vector", lambda E: E.drain())

    def barrier():
        vs = {"mm": S.val("mm")}
        for eng, sem in (("vector", "vec"), ("scalar", "act")):
            s, v = S.inc(sem)
            S.emit(eng, lambda E, s=s: E.nop(nofuse=True).then_inc(s, 1))
            vs[sem] = v
        for sem in list(S.counts):
            if sem not in vs:
                vs[sem] = S.val(sem)
        for eng in ("sync", "tensor", "vector", "scalar", "gpsimd"):
            for sem, v in vs.items():
                S.wait(eng, sem, v)

    def _emit_block():
        with nc.Block() as block:
            @block.sync
            def _(E):
                for f in S.items["sync"]:
                    f(E)

            @block.tensor
            def _(E):
                for f in S.items["tensor"]:
                    f(E)

            @block.vector
            def _(E):
                for f in S.items["vector"]:
                    f(E)

            @block.scalar
            def _(E):
                for f in S.items["scalar"]:
                    f(E)

            @block.gpsimd
            def _(E):
                for f in S.items["gpsimd"]:
                    f(E)

    PS_big = Res(S)
    PS_mid0 = Res(S)
    PS_mid1 = Res(S)
    PS_a = Res(S)
    PS_b = Res(S)
    PS_big2 = Res(S)
    ps_big = ps_big_t.ap()
    ps_mid = ps_mid_t.ap()
    ps_a = ps_a_t.ap()
    ps_b = ps_b_t.ap()

    dma("sync", "ldp", eye_b.ap(), eyebd)
    v_eye = dma("sync", "ldp", eye_f.ap(), eyefd)
    v_eye = dma("sync", "ldp", eye_p.ap(), eyepd)
    dma("sync", "ldp", bqc.ap(), bqd)
    dma("sync", "ldp", bkc.ap(), bkd)
    vop(lambda E: E.memset(consts.ap()[:, 0:1], 1e-5), inc=False)
    vop(lambda E: E.memset(consts.ap()[:, 1:2], 1.0), inc=False)
    vop(lambda E: E.memset(spc.ap(), 1.0), inc=False)
    vop(lambda E: E.memset(lnst.ap(), 0.0), inc=False)
    vop(lambda E: E.memset(stt_.ap(), 0.0), inc=False)
    v_consts = vop(lambda E: E.memset(ones_row.ap(), 1.0))
    for _eng in ("tensor", "vector", "scalar"):
        S.wait(_eng, "ldp", S.val("ldp"))
    S.wait("tensor", "vec", v_consts)
    S.wait("scalar", "vec", v_consts)

    w1f = w1bd.rearrange("a b c -> (a b c)").rearrange("(n p m) -> n p m", p=128, m=P1T)
    w2f = w2bd.rearrange("a b c -> (a b c)").rearrange("(n p m) -> n p m", p=128, m=P1T)
    w3f = w3bd.rearrange("a b c -> (a b c)").rearrange("(n p m) -> n p m", p=128, m=P1T)
    t1f = t1_scr.rearrange("(n p m) -> n p m", p=128, m=P1T)
    t2f = t2_scr.rearrange("(n p m) -> n p m", p=128, m=P1T)
    t3f = t3_scr.rearrange("(n p m) -> n p m", p=128, m=P1T)
    wsrc = [w1f, w2f, w3f]
    tdst = [t1f, t2f, t3f]

    bg_state = dict(tile_act={}, pass2_store={}, tern_done={}, fin_evac=0)

    def bg_steps():
        tile_act = bg_state["tile_act"]
        pass2_store = bg_state["pass2_store"]
        for m in range(3):
            for tn in range(NTM):
                idx = m * NTM + tn
                par = idx % 2
                buf = ws.ap()[:, par * P1T:(par + 1) * P1T]

                def step(m=m, tn=tn, idx=idx, buf=buf, par=par):
                    if idx >= 2:
                        S.wait("gpsimd", "act", tile_act[idx - 2])
                    vldw = dma("gpsimd", "gld%d" % par, buf, wsrc[m][tn])
                    S.wait("scalar", "gld%d" % par, vldw)
                    tile_act[idx] = aop(lambda E, b=buf, c=idx: E.activation(
                        ta_b.ap()[:, 0:P1T], b, AF.Abs,
                        accum_out=p1a.ap()[:, c:c + 1]))
                yield step
        for m in range(3):
            def fin(m=m):
                S.wait("vector", "act", bg_state["tile_act"][(m + 1) * NTM - 1])
                v1 = vop(lambda E, m=m: E.tensor_reduce(
                    stt_.ap()[:, 8:9], p1a.ap()[:, m * NTM:(m + 1) * NTM], AX.X, ALU.add))
                PS_a.write("tensor")
                S.wait("tensor", "vec", v1)
                mt = mmi(ps_a[0:1, 0:1], stt_.ap()[:, 8:9], consts.ap()[:, 1:2],
                         True, True, inc=True)
                PS_a.wrote("mm", mt)
                PS_a.read("vector")
                v2 = vop(lambda E: E.tensor_copy(stt_.ap()[0:1, 9:10], ps_a[0:1, 0:1]))
                PS_a.did_read("vec", v2)
                PS_a.write("tensor")
                S.wait("tensor", "vec", v2)
                mt2 = mmi(ps_a[:, 1:2], ones_row.ap()[0:1, 0:128], stt_.ap()[0:1, 9:10],
                          True, True, inc=True)
                PS_a.wrote("mm", mt2)
                PS_a.read("vector")
                v3 = vop(lambda E, m=m: E.tensor_scalar(
                    wmean.ap()[:, m:m + 1], ps_a[:, 1:2], 1.0 / W_N, 1e-5,
                    ALU.mult, ALU.max))
                PS_a.did_read("vec", v3)
                S.wait("scalar", "vec", v3)
                aop(lambda E, m=m: E.activation(
                    wmean.ap()[:, 3 + m:4 + m], wmean.ap()[:, m:m + 1], AF.Identity,
                    scale=0.5), inc=False)
                v_th = aop(lambda E, m=m: E.activation(
                    wmean.ap()[:, 6 + m:7 + m], wmean.ap()[:, m:m + 1], AF.Identity,
                    scale=-0.5))
                S.wait("vector", "act", v_th)
            yield fin
            for tn in range(NTM):
                par = tn % 2
                buf = ws.ap()[:, par * P1T:(par + 1) * P1T]

                def tern(m=m, tn=tn, buf=buf, par=tn % 2):
                    pass2_store = bg_state["pass2_store"]
                    if m == 0 and tn < 2:
                        S.wait("gpsimd", "act", bg_state["tile_act"][3 * NTM - 1])
                    else:
                        prev = pass2_store.get((m, tn - 2), pass2_store.get((m - 1, NTM + tn - 2)))
                        if prev:
                            S.wait("gpsimd", "gstt", prev)
                    vldw = dma("gpsimd", "gld%d" % par, buf, wsrc[m][tn])
                    S.wait("vector", "gld%d" % par, vldw)
                    vop(lambda E, b=buf, m=m: E.tensor_scalar(
                        ta_b.ap()[:, 0:P1T], b, wmean.ap()[:, 3 + m:4 + m], None,
                        ALU.is_ge), inc=False)
                    vt = vop(lambda E, b=buf, m=m: E.scalar_tensor_tensor(
                        tt_b.ap()[:, 0:P1T], b, wmean.ap()[:, 6 + m:7 + m],
                        ta_b.ap()[:, 0:P1T], ALU.is_le, ALU.subtract))
                    S.wait("gpsimd", "vec", vt)
                    pass2_store[(m, tn)] = dma("gpsimd", "gstt", tdst[m][tn],
                                               tt_b.ap()[:, 0:P1T])
                    if tn == NTM - 1:
                        bg_state["tern_done"][m] = pass2_store[(m, tn)]
                yield tern

    bg_iter = bg_steps()
    bg_empty = [False]

    def drain_bg(k):
        for _ in range(k):
            stp = next(bg_iter, None)
            if stp is None:
                bg_empty[0] = True
                return
            stp()

    def drain_all_bg():
        while not bg_empty[0]:
            drain_bg(1)


    def xs_view(par):
        return xs.ap()[:, par * KC * 256:(par + 1) * KC * 256]

    KFM = Res(S)
    VTM = Res(S)
    QH = [Res(S), Res(S)]
    QGB = [Res(S), Res(S)]
    WSL = Res(S)
    XSB = [Res(S), Res(S)]
    ESB = [Res(S) for _ in range(4)]
    ETS = [Res(S), Res(S)]
    OSB = [Res(S), Res(S)]
    MSK = Res(S)
    BVB = Res(S)
    BVR = Res(S)
    BOR = Res(S)

    out_store_final = 0

    prev_head = [None]

    def emit_pv(ph):
        hg, h, j, nk, eoa, eob, gh, vnorm, opar = ph
        pv_stop = 0
        last_tr = 0
        for kc2 in range(nk):
            tpar = kc2 % 2
            slot = ps_a if tpar == 0 else ps_b
            slotR = PS_a if tpar == 0 else PS_b
            slotR.write("tensor")
            S.wait("tensor", "vec", vnorm)
            tri(slot[:, 0:128], e_sb.ap()[:, eoa + kc2 * 128:eoa + kc2 * 128 + 128],
                eye_p.ap())
            t_mm = tri(slot[:, 128:256], e_sb.ap()[:, eob + kc2 * 128:eob + kc2 * 128 + 128],
                       eye_p.ap(), inc=True)
            slotR.wrote("mm", t_mm)
            slotR.read("vector")
            ETS[tpar].write("vector")
            ev = vop(lambda E, tp=tpar, sl=slot: E.tensor_copy(
                eT_sb.ap()[:, tp * 256:tp * 256 + 256], sl[:, 0:256]))
            slotR.did_read("vec", ev)
            ETS[tpar].wrote("vec", ev)
            ETS[tpar].read("tensor")
            if kc2 == 0:
                PS_mid1.write("tensor")
            vt = v_tm.ap()[:, kc2 * 256 + h * 128:kc2 * 256 + h * 128 + 128]
            pv = mmi(ps_mid[:, 512:512 + 256], vt,
                     eT_sb.ap()[:, tpar * 256:tpar * 256 + 256],
                     kc2 == 0, kc2 == nk - 1, inc=(kc2 == nk - 1))
            ETS[tpar].did_read("mm", pv if pv else S.val("mm"))
            if pv:
                pv_stop = pv
        PS_mid1.wrote("mm", pv_stop)
        VTM.did_read("mm", pv_stop)
        ESB[(h % 2) * 2].did_read("mm", pv_stop)
        ESB[(h % 2) * 2 + 1].did_read("mm", pv_stop)
        PS_mid1.read("vector")
        OSB[opar].write("vector")
        ve = vop(lambda E, op=opar: E.tensor_copy(
            o_sb.ap()[:, op * 256:op * 256 + 256], ps_mid[:, 512:512 + 256]))
        PS_mid1.did_read("vec", ve)
        OSB[opar].wrote("vec", ve)
        OSB[opar].read("gpsimd")
        vst = dma("gpsimd", "gso%d" % opar,
                  out_scr[gh * 128:(gh + 1) * 128, 2 * j * 128:2 * j * 128 + 256],
                  o_sb.ap()[:, opar * 256:opar * 256 + 256])
        OSB[opar].did_read("gso%d" % opar, vst)
        return pv_stop, vst

    opar_ctr = 0

    WQ2 = [Res(S), Res(S)]
    QEV = Res(S)
    q_ev = bvbc
    q_store_final = 0
    for xqb in range(4):
        par = xqb % 2
        XSB[par].write("sync")
        v_x = dma("sync", "ldx%d" % par, xs_view(par), x_qb[xqb])
        XSB[par].wrote("ldx%d" % par, v_x)
        for oc in range(KC):
            wpar = oc % 2
            WQ2[wpar].write("sync")
            v_w = dma("sync", "ldwq%d" % wpar,
                      wq_s.ap()[:, wpar * 8 * KB // 4:wpar * 8 * KB // 4 + D]
                      if False else wq_s.ap()[:, wpar * D:wpar * D + D], Wqb2[oc])
            WQ2[wpar].wrote("ldwq%d" % wpar, v_w)
            psR = PS_a if (oc % 2 == 0) else PS_b
            ps = ps_a if (oc % 2 == 0) else ps_b
            psR.write("tensor")
            XSB[par].read("tensor")
            WQ2[wpar].read("tensor")
            for kc in range(KC):
                lh = wq_s.ap()[:, wpar * D + kc * 128:wpar * D + kc * 128 + 128]
                rh = xs_view(par)[:, kc * 256:kc * 256 + 256]
                last = kc == KC - 1
                vmm = mmi(ps[:, 0:256], lh, rh, kc == 0, last, inc=last)
            psR.wrote("mm", vmm)
            XSB[par].did_read("mm", vmm)
            WQ2[wpar].did_read("mm", vmm)
            QEV.write("vector")
            psR.read("vector")
            vq = vop(lambda E, pz=ps, oc=oc: E.tensor_scalar(
                q_ev.ap(), pz[:, 0:256], bqc.ap()[:, oc:oc + 1], None, ALU.add))
            psR.did_read("vec", vq)
            QEV.wrote("vec", vq)
            QEV.read("gpsimd")
            for hh_l in range(4):
                gh2 = xqb * 4 + hh_l
                q_store_final = dma("gpsimd", "gsq",
                                    q_scr[gh2, :, oc * 64:oc * 64 + 64],
                                    q_ev.ap()[:, hh_l * 64:hh_l * 64 + 64])
            QEV.did_read("gsq", q_store_final)

    for hg in range(NHG):
        drain_bg(2)
        WSL.write("sync")
        sl = hg * 256
        dma("sync", "ldw", wk_s.ap(), Wkb[hg])
        v_wsl = dma("sync", "ldw", wv_s.ap(), Wvb[hg])
        WSL.wrote("ldw", v_wsl)
        BVR.write("sync")
        v_bvr = dma("sync", "ldbv", bv_row.ap(), bvd[0:1, sl:sl + 256])
        BVR.wrote("ldbv", v_bvr)
        PS_mid0.write("tensor")
        WSL.read("tensor")
        BVR.read("tensor")
        S.wait("tensor", "vec", v_consts)
        mm_bv = mmi(ps_mid[:, 0:256], ones_row.ap()[0:1, 0:128],
                    bv_row.ap()[0:1, 0:256], True, True, inc=True)
        BVR.did_read("mm", mm_bv)
        PS_mid0.wrote("mm", mm_bv)
        BVB.write("vector")
        PS_mid0.read("vector")
        v_bvbc = vop(lambda E: E.tensor_copy(bvbc.ap(), ps_mid[:, 0:256]))
        PS_mid0.did_read("vec", v_bvbc)
        BVB.wrote("vec", v_bvbc)

        kv_evac = 0
        for tb in range(8):
            par = tb % 2
            XSB[par].write("sync")
            if tb < 8:
                v_x = dma("sync", "ldx%d" % par, xs_view(par), x_kvb[tb])
            else:
                v_x = dma("sync", "ldx%d" % par, xs_view(par), x_qb[tb - 8])
            XSB[par].wrote("ldx%d" % par, v_x)
            if tb < 8:
                for oc in range(2):
                    psR = PS_mid0 if oc == 0 else PS_mid1
                    ps = ps_mid[:, 0:256] if oc == 0 else ps_mid[:, 512:512 + 256]
                    psR.write("tensor")
                    XSB[par].read("tensor")
                    WSL.read("tensor")
                    for kc in range(KC):
                        lh = wk_s.ap()[:, kc * 256 + oc * 128:kc * 256 + oc * 128 + 128]
                        rh = xs_view(par)[:, kc * 256:kc * 256 + 256]
                        last = kc == KC - 1
                        vmm = mmi(ps, lh, rh, kc == 0, last, inc=last)
                    psR.wrote("mm", vmm)
                    XSB[par].did_read("mm", vmm)
                    WSL.did_read("mm", vmm)
                    gc = hg * 2 + oc
                    KFM.write("vector")
                    psR.read("vector")
                    kv_evac = vop(lambda E, pz=ps, oc=oc, tb=tb, gc=gc: E.tensor_scalar(
                        k_fm.ap()[:, oc * T + tb * 256:oc * T + tb * 256 + 256],
                        pz, bkc.ap()[:, gc:gc + 1], None, ALU.add))
                    psR.did_read("vec", kv_evac)
                    KFM.wrote("vec", kv_evac)
                for tc in range(2):
                    psR = PS_a if tc == 0 else PS_b
                    ps = ps_a if tc == 0 else ps_b
                    psR.write("tensor")
                    XSB[par].read("tensor")
                    WSL.read("tensor")
                    for kc in range(KC):
                        lh = xs_view(par)[:, kc * 256 + tc * 128:kc * 256 + tc * 128 + 128]
                        rh = wv_s.ap()[:, kc * 256:kc * 256 + 256]
                        last = kc == KC - 1
                        vmm = mmi(ps[:, 0:256], lh, rh, kc == 0, last, inc=last)
                    psR.wrote("mm", vmm)
                    XSB[par].did_read("mm", vmm)
                    WSL.did_read("mm", vmm)
                    VTM.write("vector")
                    BVB.read("vector")
                    psR.read("vector")
                    kv_evac = vop(lambda E, pz=ps, tb=tb, tc=tc: E.tensor_tensor(
                        v_tm.ap()[:, (tb * 2 + tc) * 256:(tb * 2 + tc) * 256 + 256],
                        pz[:, 0:256], bvbc.ap(), ALU.add))
                    psR.did_read("vec", kv_evac)
                    VTM.wrote("vec", kv_evac)
            drain_bg(1)
        v_p1_done = kv_evac

        for j in range(4):
            L = 1024 + 256 * (j + 1)
            nk = L // 128
            MSK.write("sync")
            dma("sync", "ldm", msk_s.ap()[:, 0:L],
                maskd[2 * j * 128:2 * j * 128 + 128, 0:L])
            v_msk = dma("sync", "ldm", msk_s.ap()[:, T:T + L],
                        maskd[(2 * j + 1) * 128:(2 * j + 1) * 128 + 128, 0:L])
            MSK.wrote("ldm", v_msk)
            for h in range(HG):
                hpar = h % 2
                gh = hg * HG + h
                vnorms = [0, 0]
                QH[hpar].write("sync")
                if hg == 0 and j == 0 and h == 0:
                    S.wait("sync", "gsq", q_store_final)
                v_qh = dma("sync", "ldq%d" % hpar,
                           q_h.ap()[:, hpar * 1024:hpar * 1024 + 1024], q_scr[gh])
                QH[hpar].wrote("ldq%d" % hpar, v_qh)
                for qq in range(2):
                    qb_ = 2 * j + qq
                    eoff = hpar * 2 * T + qq * T
                    eres = ESB[hpar * 2 + qq]
                    PS_big.write("tensor")
                    MSK.read("tensor")
                    KFM.read("tensor")
                    S.wait("tensor", "vec", v_p1_done)
                    nsl = (L + 511) // 512
                    qg3 = q_h.ap()[:, hpar * 1024:hpar * 1024 + 1024] \
                        .rearrange("p (c u) -> p c u", u=64)[:, :, qb_ * 8:qb_ * 8 + 8]
                    QGB[qq].write("vector")
                    QH[hpar].read("vector")
                    v_qg = vop(lambda E, qg3=qg3, qq=qq: E.tensor_copy(
                        qgb.ap()[:, qq * 128:qq * 128 + 128]
                        .rearrange("p (c u) -> p c u", u=8), qg3))
                    QH[hpar].did_read("vec", v_qg)
                    QGB[qq].wrote("vec", v_qg)
                    QGB[qq].read("tensor")
                    for ksl in range(nsl):
                        w = min(512, L - ksl * 512)
                        lh = qgb.ap()[:, qq * 128:qq * 128 + 128]
                        rh = k_fm.ap()[:, h * T + ksl * 512:h * T + ksl * 512 + w]
                        mmi(ps_big[:, ksl * 512:ksl * 512 + w], lh, rh, True, False)
                        mrh = msk_s.ap()[:, qq * T + ksl * 512:qq * T + ksl * 512 + w]
                        sc_mm = mmi(ps_big[:, ksl * 512:ksl * 512 + w], eye_b.ap(), mrh,
                                    False, True, inc=(ksl == nsl - 1))
                    PS_big.wrote("mm", sc_mm)
                    KFM.did_read("mm", sc_mm)
                    QGB[qq].did_read("mm", sc_mm)
                    MSK.did_read("mm", sc_mm)
                    PS_big.read("vector")
                    vmax = vop(lambda E, L=L: E.tensor_reduce(
                        stt_.ap()[:, 0:1], ps_big[:, 0:L], AX.X, ALU.max))
                    PS_big.did_read("vec", vmax)
                    vdrain()
                    vbias = vop(lambda E: E.tensor_scalar(
                        stt_.ap()[:, 1:2], stt_.ap()[:, 0:1], -SCL, None, ALU.mult))
                    S.wait("scalar", "vec", vbias)
                    eres.write("scalar")
                    PS_big.read("scalar")
                    vexp = aop(lambda E, eo=eoff, L=L: E.activation(
                        e_sb.ap()[:, eo:eo + L], ps_big[:, 0:L], AF.Exp,
                        bias=stt_.ap()[:, 1:2], scale=SCL,
                        accum_out=stt_.ap()[:, 2:3]))
                    PS_big.did_read("act", vexp)
                    aop(lambda E: E.activation(stt_.ap()[:, 3:4], stt_.ap()[:, 2:3],
                                               AF.Ln), inc=False)
                    vrs = aop(lambda E: E.activation(stt_.ap()[:, 4:5], stt_.ap()[:, 3:4],
                                                     AF.Exp, scale=-1.0))
                    S.wait("vector", "act", vrs)
                    vnorms[qq] = vop(lambda E, eo=eoff, L=L: E.tensor_scalar(
                        e_sb.ap()[:, eo:eo + L], e_sb.ap()[:, eo:eo + L],
                        stt_.ap()[:, 4:5], None, ALU.mult))
                    eres.wrote("vec", vnorms[qq])
                    if qq == 0 and prev_head[0] is not None:
                        _, out_store_final = emit_pv(prev_head[0])
                        prev_head[0] = None
                if prev_head[0] is not None:
                    _, out_store_final = emit_pv(prev_head[0])
                prev_head[0] = (hg, h, j, nk, hpar * 2 * T, hpar * 2 * T + T, gh,
                                vnorms[1], opar_ctr % 2)
                opar_ctr += 1
                drain_bg(1)
        if prev_head[0] is not None:
            _, out_store_final = emit_pv(prev_head[0])
            prev_head[0] = None
    def early_out():
        vz = vop(lambda E: E.memset(x1_sb.ap(), 0.5))
        S.wait("sync", "vec", vz)
        vst = dma("sync", "sty", y_out[0:128, :], x1_sb.ap())
        for sem in list(S.counts):
            if sem.startswith(("sty", "stx1", "gstt", "gso")):
                S.wait("sync", sem, S.val(sem))
        _emit_block()
        return nc

    if STOP_AFTER == "attn_nobg":
        return early_out()
    drain_all_bg()
    if STOP_AFTER in ("pass", "attn"):
        return early_out()

    barrier()
    WOS = [Res(S), Res(S)]
    OSS = [Res(S), Res(S)]
    O3 = Res(S)
    XQS = [Res(S) for _ in range(4)]
    R1C = [Res(S) for _ in range(8)]
    X1SB = Res(S)
    QKR = Res(S)
    XQFM = Res(S)

    for s4 in range(4):
        BOR.write("sync")
        v_bor = dma("sync", "ldbo", bo_row.ap(), bod[0:1, s4 * 512:s4 * 512 + 512])
        BOR.wrote("ldbo", v_bor)
        PS_mid0.write("tensor")
        BOR.read("tensor")
        mm_bo = mmi(ps_mid[:, 0:512], ones_row.ap()[0:1, 0:128],
                    bo_row.ap()[0:1, 0:512], True, True, inc=True)
        BOR.did_read("mm", mm_bo)
        PS_mid0.wrote("mm", mm_bo)
        PS_mid0.read("vector")
        v_bo_bc = vop(lambda E, s4=s4: E.tensor_copy(
            bo_bc.ap()[:, s4 * 512:s4 * 512 + 512], ps_mid[:, 0:512]))
        PS_mid0.did_read("vec", v_bo_bc)
    if DEBUG_DUMP:
        S.wait("sync", "vec", v_bo_bc)
        dma("sync", "std", bo_dbg, bo_bc.ap())

    p3_ps = [ps_mid[:, 0:512], ps_mid[:, 512:1024], ps_a[:, 0:512], ps_b[:, 0:512]]
    p3_R = [PS_mid0, PS_mid1, PS_a, PS_b]
    S.wait("sync", "gso0", S.val("gso0"))
    S.wait("sync", "gso1", S.val("gso1"))

    for tokb in range(2):
        for tt in range(4):
            tc = tokb * 4 + tt
            XQS[tt].write("sync")
            v_xq = dma("sync", "ldxq", xqs.ap()[:, tt * D:tt * D + D],
                       x_q_tm[tc * 128:tc * 128 + 128, :])
            XQS[tt].wrote("ldxq", v_xq)
        for dmg in range(4):
            for dmc in range(4):
                p3_R[dmc].write("tensor")
            for kc in range(KC):
                par = kc % 2
                WOS[par].write("sync")
                v_wo = dma("sync", "ldwo%d" % par, wo_s.ap()[:, par * 512:par * 512 + 512],
                           Wod[kc * 128:kc * 128 + 128, dmg * 512:dmg * 512 + 512])
                WOS[par].wrote("ldwo%d" % par, v_wo)
                OSS[par].write("sync")
                v_os = dma("sync", "ldos%d" % par, os_s.ap()[:, par * 512:par * 512 + 512],
                           out_scr[kc * 128:kc * 128 + 128, tokb * 512:tokb * 512 + 512])
                OSS[par].wrote("ldos%d" % par, v_os)
                WOS[par].read("tensor")
                OSS[par].read("tensor")
                for dmc in range(4):
                    lh = wo_s.ap()[:, par * 512 + dmc * 128:par * 512 + dmc * 128 + 128]
                    last = kc == KC - 1
                    gmm = mmi(p3_ps[dmc], lh, os_s.ap()[:, par * 512:par * 512 + 512],
                              kc == 0, last, inc=(dmc == 3))
                WOS[par].did_read("mm", gmm)
                OSS[par].did_read("mm", gmm)
            v_grp = S.val("mm")
            for dmc in range(4):
                p3_R[dmc].wrote("mm", v_grp)
            for dmc in range(4):
                O3.write("vector")
                p3_R[dmc].read("vector")
                ve1 = vop(lambda E, d=dmc: E.tensor_copy(o3_sb.ap(), p3_ps[d]))
                p3_R[dmc].did_read("vec", ve1)
                O3.wrote("vec", ve1)
                for tt in range(4):
                    tc = tokb * 4 + tt
                    PS_big.write("tensor")
                    O3.read("tensor")
                    tmm = tri(ps_big[:, tt * 128:tt * 128 + 128],
                              o3_sb.ap()[:, tt * 128:tt * 128 + 128], eye_f.ap(),
                              inc=True)
                    PS_big.wrote("mm", tmm)
                    O3.did_read("mm", tmm)
                    dmcol = dmg * 512 + dmc * 128
                    XQS[tt].read("vector")
                    PS_big.read("vector")
                    R1C[tc].write("vector")
                    vr1 = vop(lambda E, tt=tt, tc=tc, dmcol=dmcol: E.tensor_tensor(
                        r1.ap()[:, tc * D + dmcol:tc * D + dmcol + 128],
                        ps_big[:, tt * 128:tt * 128 + 128],
                        xqs.ap()[:, tt * D + dmcol:tt * D + dmcol + 128], ALU.add))
                    PS_big.did_read("vec", vr1)
                    XQS[tt].did_read("vec", vr1)
                    R1C[tc].wrote("vec", vr1)

        for tt in range(4):
            tc = tokb * 4 + tt
            rr = r1.ap()[:, tc * D:tc * D + D]
            if DEBUG_DUMP:
                S.wait("sync", "vec", R1C[tc].w[1])
                dma("sync", "std", r1_dbg[tc * 128:tc * 128 + 128, :], rr)
            vb = vop(lambda E, rr=rr: E.tensor_tensor(rr, rr, bo_bc.ap(), ALU.add))
            vop(lambda E, rr=rr: E.tensor_reduce(lnst.ap()[:, 0:1], rr, AX.X, ALU.add),
                inc=False)
            vdrain()
            vmean = vop(lambda E: E.tensor_scalar(
                lnst.ap()[:, 1:2], lnst.ap()[:, 0:1], -1.0 / D, None, ALU.mult))
            vdrain()
            S.wait("scalar", "vec", vmean)
            if DEBUG_DUMP:
                S.wait("sync", "vec", vmean)
                v_lnd0 = dma("sync", "std", ln_dbg[:, tc * 32:tc * 32 + 16], lnst.ap())
                S.wait("vector", "std", v_lnd0)
                S.wait("scalar", "std", v_lnd0)
            vsq = aop(lambda E, rr=rr: E.activation(
                sq_scr.ap(), rr, AF.Square, accum_out=lnst.ap()[:, 2:3]))
            S.wait("vector", "act", vsq)
            vop(lambda E: E.tensor_tensor(
                lnst.ap()[:, 3:4], lnst.ap()[:, 1:2], lnst.ap()[:, 1:2], ALU.mult),
                inc=False)
            vdrain()
            vvar = vop(lambda E: E.scalar_tensor_tensor(
                lnst.ap()[:, 4:5], lnst.ap()[:, 2:3], 1.0 / D, lnst.ap()[:, 3:4],
                ALU.mult, ALU.subtract))
            vdrain()
            S.wait("scalar", "vec", vvar)
            aop(lambda E: E.activation(lnst.ap()[:, 5:6], lnst.ap()[:, 4:5], AF.Ln,
                                       bias=consts.ap()[:, 0:1]), inc=False)
            vrstd = aop(lambda E: E.activation(lnst.ap()[:, 6:7], lnst.ap()[:, 5:6],
                                               AF.Exp, scale=-0.5))
            S.wait("vector", "act", vrstd)
            vop(lambda E: E.tensor_copy(spc.ap(), spc.ap()), inc=False)
            vdrain()
            vnmr = vop(lambda E: E.tensor_scalar(
                lnst.ap()[:, 7:8], lnst.ap()[:, 1:2], lnst.ap()[:, 6:7], None, ALU.mult))
            S.wait("scalar", "vec", vnmr)
            aop(lambda E: E.activation(spc.ap(), spc.ap(), AF.Identity), inc=False)
            X1SB.write("scalar")
            vx1 = aop(lambda E, rr=rr: E.activation(
                x1_sb.ap(), rr, AF.Identity,
                bias=lnst.ap()[:, 7:8], scale=lnst.ap()[:, 6:7]))
            X1SB.wrote("act", vx1)
            vx1 = aop(lambda E: E.activation(spc.ap(), spc.ap(), AF.Identity))
            X1SB.wrote("act", vx1)
            if DEBUG_DUMP:
                S.wait("sync", "act", vx1)
                v_lnd = dma("sync", "std", ln_dbg[:, tc * 32 + 16:tc * 32 + 32], lnst.ap())
                S.wait("vector", "std", v_lnd)
                S.wait("scalar", "std", v_lnd)
            X1SB.read("sync")
            v_x1st = dma("sync", "stx1", x1_scr[tc * 128:tc * 128 + 128, :], x1_sb.ap())
            X1SB.did_read("stx1", v_x1st)
            S.wait("vector", "act", vx1)
            vop(lambda E: E.tensor_reduce(
                lnst.ap()[:, 8:9], x1_sb.ap(), AX.X, ALU.max,
                apply_absolute_value=True), inc=False)
            vdrain()
            vaxc = vop(lambda E, tc=tc: E.tensor_scalar(
                sxa.ap()[:, tc:tc + 1], lnst.ap()[:, 8:9], 1e-5, None, ALU.max))
            vdrain()
            S.wait("scalar", "vec", vaxc)
            aop(lambda E, tc=tc: E.activation(
                lnst.ap()[:, 9:10], sxa.ap()[:, tc:tc + 1], AF.Ln), inc=False)
            vsx = aop(lambda E: E.activation(lnst.ap()[:, 10:11], lnst.ap()[:, 9:10],
                                             AF.Exp, scale=-1.0))
            S.wait("vector", "act", vsx)
            vop(lambda E: E.tensor_copy(qd.ap(), x1_sb.ap()[:, 0:512]), inc=False)
            vdrain()
            for cb in range(4):
                xsl = x1_sb.ap()[:, cb * 512:cb * 512 + 512]
                QKR.write("vector")
                vop(lambda E, xsl=xsl: E.tensor_scalar(
                    qy.ap(), xsl, lnst.ap()[:, 10:11], 127.0, ALU.mult, ALU.mult),
                    inc=False)
                vop(lambda E: E.tensor_copy(qi.ap(), qy.ap()), inc=False)
                vop(lambda E: E.tensor_tensor(qd.ap(), qy.ap(), qi.ap(), ALU.subtract),
                    inc=False)
                vop(lambda E: E.scalar_tensor_tensor(
                    qa.ap(), qd.ap(), 0.5, qi.ap(), ALU.is_gt, ALU.add), inc=False)
                vop(lambda E: E.tensor_scalar(qb.ap(), qd.ap(), -0.5, None, ALU.is_lt),
                    inc=False)
                vop(lambda E: E.tensor_tensor(qa.ap(), qa.ap(), qb.ap(), ALU.subtract),
                    inc=False)
                vqk = vop(lambda E: E.tensor_scalar(
                    qy.ap(), qa.ap(), -128.0, 127.0, ALU.max, ALU.min))
                QKR.wrote("vec", vqk)
                for s2 in range(4):
                    kcg = cb * 4 + s2
                    PS_a.write("tensor")
                    QKR.read("tensor")
                    tmm = tri(ps_a[:, 0:128], qy.ap()[:, s2 * 128:s2 * 128 + 128],
                              eye_f.ap(), inc=True)
                    PS_a.wrote("mm", tmm)
                    QKR.did_read("mm", tmm)
                    PS_a.read("vector")
                    XQFM.write("vector")
                    vxq = vop(lambda E, kcg=kcg, tc=tc: E.tensor_copy(
                        xq_fm.ap()[:, kcg * TQ + tc * 128:kcg * TQ + tc * 128 + 128],
                        ps_a[:, 0:128]))
                    PS_a.did_read("vec", vxq)
                    XQFM.wrote("vec", vxq)

    if STOP_AFTER == "p3":
        return early_out()
    if STOP_AFTER == "p3pad":
        for _ in range(200):
            vop(lambda E: E.nop(), inc=False)
        return early_out()
    for tc in range(8):
        vop(lambda E, tc=tc: E.tensor_scalar(
            dq1.ap()[:, tc:tc + 1], sxa.ap()[:, tc:tc + 1], wmean.ap()[:, 0:1],
            -1.0 / 127.0, ALU.mult, ALU.mult), inc=False)
        vop(lambda E, tc=tc: E.tensor_scalar(
            dq2.ap()[:, tc:tc + 1], sxa.ap()[:, tc:tc + 1], wmean.ap()[:, 1:2],
            -1.0 / 127.0, ALU.mult, ALU.mult), inc=False)
    v_dq = vop(lambda E: E.nop(nofuse=True))
    if STOP_AFTER == "p3dq":
        return early_out()
    if STOP_AFTER == "bar_t2":
        s, v = S.inc("mm")
        S.emit("tensor", lambda E, s=s: E.nop(nofuse=True).then_inc(s, 1))
        return early_out()
    if STOP_AFTER == "bar_t3":
        for e2 in ("sync", "vector", "scalar", "gpsimd"):
            S.wait(e2, "mm", S.val("mm"))
        return early_out()
    if STOP_AFTER in ("bar_t", "bar_v", "bar_s"):
        eng, sem = {"bar_t": ("tensor", "mm"), "bar_v": ("vector", "vec"),
                    "bar_s": ("scalar", "act")}[STOP_AFTER]
        s, v = S.inc(sem)
        S.emit(eng, lambda E, s=s: E.nop(nofuse=True).then_inc(s, 1))
        for e2 in ("sync", "tensor", "vector", "scalar", "gpsimd"):
            S.wait(e2, sem, v)
        return early_out()
    if STOP_AFTER == "p3bar_c":
        vs = {}
        for eng, sem in (("tensor", "mm"), ("vector", "vec"), ("scalar", "act")):
            s, v = S.inc(sem)
            S.emit(eng, lambda E, s=s: E.nop(nofuse=True).then_inc(s, 1))
            vs[sem] = v
        for eng in ("sync", "tensor", "vector", "scalar", "gpsimd"):
            for sem, v in vs.items():
                S.wait(eng, sem, v)
        return early_out()

    barrier()
    if DEBUG_DUMP:
        dma("sync", "std", xq_dbg, xq_fm.ap())
    if STOP_AFTER == "p5bar":
        return early_out()
    t1v = t1_scr.rearrange("(b p m) -> b p m", p=128, m=KC * 512)
    t2v = t2_scr.rearrange("(b p m) -> b p m", p=128, m=KC * 512)
    t3v = t3_scr.rearrange("(b p m) -> b p m", p=128, m=(HID // 128) * 128)

    TS = [Res(S), Res(S)]
    HQ1 = [Res(S), Res(S)]
    HQ2 = [Res(S), Res(S)]
    HQT = [Res(S), Res(S)]
    HQF = Res(S)
    T3S = [Res(S), Res(S)]
    R2C = [Res(S) for _ in range(4)]
    FRS = Res(S)
    YSB = Res(S)
    QKR2 = Res(S)

    p5_ps = {("g1", 0): (ps_a[:, 0:512], PS_a), ("g1", 1): (ps_mid[:, 0:512], PS_mid0),
             ("g2", 0): (ps_b[:, 0:512], PS_b), ("g2", 1): (ps_mid[:, 512:1024], PS_mid1)}
    p6_ps = [(ps_mid[:, 0:512], PS_mid0), (ps_mid[:, 512:1024], PS_mid1),
             (ps_big[:, 0:512], PS_big), (ps_big[:, 512:1024], PS_big)]
    p6_tr = [(ps_big[:, 1024 + i * 128:1024 + (i + 1) * 128], PS_big2) for i in range(4)]

    y_store = 0
    for half in range(2):
        if half == 1:
            barrier()
        for qq in range(2):
            tc0 = half * 4 + qq * 2
            for ob in range(11):
                if STOP_AFTER == "p5ob%d" % ob and half == 0 and qq == 0 and ob > 0:
                    return early_out()
                for m, tv, dqv in ((0, t1v, dq1), (1, t2v, dq2)):
                    par = (ob * 2 + m) % 2
                    TS[par].write("sync")
                    S.wait("sync", "gstt", bg_state["tern_done"][m])
                    v_t = dma("sync", "ldt%d" % par,
                              t_s.ap()[:, par * KC * 512:(par + 1) * KC * 512],
                              tv[ob])
                    TS[par].wrote("ldt%d" % par, v_t)
                    if STOP_AFTER == "p5a" and half == 0 and qq == 0 and ob == 0 and m == 0:
                        return early_out()
                    for tc in range(2):
                        tcg = tc0 + tc
                        key = ("g1" if m == 0 else "g2", tc)
                        psv, psR = p5_ps[key]
                        psR.write("tensor")
                        TS[par].read("tensor")
                        XQFM.read("tensor")
                        if half == 0 and qq == 0 and ob == 0 and m == 0 and tc == 0:
                            S.wait("tensor", "vec", v_dq)
                        for kc in range(KC):
                            lh = xq_fm.ap()[:, kc * TQ + tcg * 128:kc * TQ + tcg * 128 + 128]
                            rh = t_s.ap()[:, par * KC * 512 + kc * 512:
                                          par * KC * 512 + kc * 512 + 512]
                            last = kc == KC - 1
                            vmm = mmi(psv, lh, rh, kc == 0, last, inc=last)
                        psR.wrote("mm", vmm)
                        TS[par].did_read("mm", vmm)
                        XQFM.did_read("mm", vmm)
                        if STOP_AFTER == "p5b" and half == 0 and qq == 0 and ob == 0 and tc == 0:
                            return early_out()
                        if m == 0:
                            HQ1[tc].write("scalar")
                            psR.read("scalar")
                            va = aop(lambda E, pz=psv, tcg=tcg, tc=tc: E.activation(
                                hs1.ap()[:, tc * 512:tc * 512 + 512], pz, AF.Silu,
                                scale=dq1.ap()[:, tcg:tcg + 1]))
                            psR.did_read("act", va)
                            HQ1[tc].wrote("act", va)
                            if STOP_AFTER == "p5c" and half == 0 and qq == 0 and ob == 0 and tc == 0:
                                return early_out()
                        else:
                            HQ2[tc].write("vector")
                            psR.read("vector")
                            ve2 = vop(lambda E, pz=psv, tcg=tcg, tc=tc: E.tensor_scalar(
                                hs2.ap()[:, tc * 512:tc * 512 + 512], pz,
                                dq2.ap()[:, tcg:tcg + 1], None, ALU.mult))
                            psR.did_read("vec", ve2)
                            HQ2[tc].wrote("vec", ve2)
                            HQ1[tc].read("vector")
                            HQT[tc].write("vector")
                            vh = vop(lambda E, tc=tc, ob=ob: E.tensor_tensor(
                                h_q.ap()[:, tc * HID + ob * 512:tc * HID + ob * 512 + 512],
                                hs1.ap()[:, tc * 512:tc * 512 + 512],
                                hs2.ap()[:, tc * 512:tc * 512 + 512], ALU.mult))
                            HQ1[tc].did_read("vec", vh)
                            HQT[tc].wrote("vec", vh)
                            vdrain()
                            if ob == 0:
                                vop(lambda E, tcg=tcg, tc=tc, ob=ob: E.tensor_reduce(
                                    hamax.ap()[:, tcg:tcg + 1],
                                    h_q.ap()[:, tc * HID + ob * 512:
                                             tc * HID + ob * 512 + 512],
                                    AX.X, ALU.max, apply_absolute_value=True), inc=False)
                            else:
                                vop(lambda E, tc=tc, ob=ob: E.tensor_reduce(
                                    lnst.ap()[:, 12:13],
                                    h_q.ap()[:, tc * HID + ob * 512:
                                             tc * HID + ob * 512 + 512],
                                    AX.X, ALU.max, apply_absolute_value=True), inc=False)
                                vdrain()
                                vop(lambda E, tcg=tcg: E.tensor_tensor(
                                    hamax.ap()[:, tcg:tcg + 1], hamax.ap()[:, tcg:tcg + 1],
                                    lnst.ap()[:, 12:13], ALU.max), inc=False)
            if STOP_AFTER == "p5mm_h%d_q%d" % (half, qq):
                return early_out()
            if DEBUG_DUMP and half == 0 and qq == 0:
                S.wait("sync", "vec", S.val("vec"))
                v_hd = dma("sync", "std", h_dbg, h_q.ap())
                S.wait("vector", "std", v_hd)
            if DEBUG_DUMP and half == 1 and qq == 1:
                S.wait("sync", "vec", S.val("vec"))
                v_hd2 = dma("sync", "std", h2_dbg, h_q.ap())
                S.wait("vector", "std", v_hd2)
            for tc in range(2):
                tcg = tc0 + tc
                vdrain()
                vaxc = vop(lambda E, tcg=tcg: E.tensor_scalar(
                    lnst.ap()[:, 8:9], hamax.ap()[:, tcg:tcg + 1], 1e-5, None, ALU.max))
                vdrain()
                S.wait("scalar", "vec", vaxc)
                aop(lambda E: E.activation(lnst.ap()[:, 9:10], lnst.ap()[:, 8:9], AF.Ln),
                    inc=False)
                vsh = aop(lambda E: E.activation(lnst.ap()[:, 10:11], lnst.ap()[:, 9:10],
                                                 AF.Exp, scale=-1.0))
                S.wait("vector", "act", vsh)
                vop(lambda E, tcg=tcg: E.tensor_scalar(
                    dq3.ap()[:, tcg:tcg + 1], lnst.ap()[:, 8:9], wmean.ap()[:, 2:3],
                    -1.0 / 127.0, ALU.mult, ALU.mult), inc=False)
                vop(lambda E, tc=tc: E.tensor_copy(
                    qd.ap(), h_q.ap()[:, tc * HID:tc * HID + 512]), inc=False)
                vdrain()
                for cb in range(11):
                    hsl = h_q.ap()[:, tc * HID + cb * 512:tc * HID + cb * 512 + 512]
                    QKR2.write("vector")
                    vop(lambda E, hsl=hsl: E.tensor_scalar(
                        qy.ap(), hsl, lnst.ap()[:, 10:11], 127.0, ALU.mult, ALU.mult),
                        inc=False)
                    vop(lambda E: E.tensor_copy(qi.ap(), qy.ap()), inc=False)
                    vop(lambda E: E.tensor_tensor(qd.ap(), qy.ap(), qi.ap(),
                                                  ALU.subtract), inc=False)
                    vop(lambda E: E.scalar_tensor_tensor(
                        qa.ap(), qd.ap(), 0.5, qi.ap(), ALU.is_gt, ALU.add), inc=False)
                    for thr in (1.5, 2.5, 3.5):
                        vop(lambda E, thr=thr: E.scalar_tensor_tensor(
                            qa.ap(), qd.ap(), thr, qa.ap(), ALU.is_gt, ALU.add),
                            inc=False)
                    vop(lambda E: E.tensor_scalar(qb.ap(), qd.ap(), -0.5, None,
                                                  ALU.is_lt), inc=False)
                    for thr in (-1.5, -2.5, -3.5):
                        vop(lambda E, thr=thr: E.scalar_tensor_tensor(
                            qb.ap(), qd.ap(), thr, qb.ap(), ALU.is_lt, ALU.add),
                            inc=False)
                    vop(lambda E: E.tensor_tensor(qa.ap(), qa.ap(), qb.ap(),
                                                  ALU.subtract), inc=False)
                    vqk = vop(lambda E: E.tensor_scalar(
                        qy.ap(), qa.ap(), -128.0, 127.0, ALU.max, ALU.min))
                    QKR2.wrote("vec", vqk)
                    if DEBUG_DUMP and half == 1 and qq == 0 and tc == 1 and cb == 3:
                        S.wait("sync", "vec", vqk)
                        dma("sync", "std", q_dbg[:, 0:512], qd.ap())
                        dma("sync", "std", q_dbg[:, 512:1024], qa.ap())
                        dma("sync", "std", q_dbg[:, 1024:1536], qy.ap())
                        v_qd2 = dma("sync", "std", q_dbg[:, 1536:2048], qb.ap())
                        S.wait("vector", "std", v_qd2)
                    for s2 in range(4):
                        ocg = cb * 4 + s2
                        PS_a.write("tensor")
                        QKR2.read("tensor")
                        tmm = tri(ps_a[:, 0:128], qy.ap()[:, s2 * 128:s2 * 128 + 128],
                                  eye_f.ap(), inc=True)
                        PS_a.wrote("mm", tmm)
                        QKR2.did_read("mm", tmm)
                        PS_a.read("vector")
                        HQF.write("vector")
                        col = ocg * 512 + (qq * 2 + tc) * 128
                        vhq = vop(lambda E, col=col: E.tensor_copy(
                            hq_fm.ap()[:, col:col + 128], ps_a[:, 0:128]))
                        PS_a.did_read("vec", vhq)
                        HQF.wrote("vec", vhq)
                HQT[tc].did_read("vec", S.val("vec"))
            if STOP_AFTER == "p5qq%d_h%d" % (qq, half):
                return early_out()

        if STOP_AFTER == "p5h%d" % half:
            return early_out()
        if DEBUG_DUMP:
            S.wait("sync", "vec", S.val("vec"))
            if half == 1:
                dma("sync", "std", ha_dbg, hamax.ap())
        if DEBUG_DUMP and half == 1:
            S.wait("sync", "vec", S.val("vec"))
            v_hqd = dma("sync", "std", hq_dbg, hq_fm.ap())
            S.wait("vector", "std", v_hqd)
            S.wait("tensor", "std", v_hqd)
        for tcz in range(4):
            R2C[tcz].write("sync")
        for tc in range(2):
            HQT[tc].write("sync")
        S.wait("sync", "stx1", S.val("stx1"))
        for cz in range(4):
            tcg0 = (half * 4 + cz) * 128
            v_r2 = dma("sync", "ldr2", r2.ap()[:, cz * D:(cz + 1) * D],
                       x1_scr[tcg0:tcg0 + 128, :])
        for tcz in range(4):
            R2C[tcz].wrote("ldr2", v_r2)
        for dmc in range(KC):
            par = dmc % 2
            T3S[par].write("sync")
            S.wait("sync", "gstt", bg_state["tern_done"][2])
            v_t3 = dma("sync", "ldt3%d" % par,
                       t3_sb.ap()[:, par * HID:(par + 1) * HID], t3v[dmc])
            T3S[par].wrote("ldt3%d" % par, v_t3)
            pslot = dmc % 4
            psv, psR = p6_ps[pslot]
            psR.write("tensor")
            T3S[par].read("tensor")
            HQF.read("tensor")
            for oc in range(HID // 128):
                lh = t3_sb.ap()[:, par * HID + oc * 128:par * HID + oc * 128 + 128]
                rh = hq_fm.ap()[:, oc * 512:oc * 512 + 512]
                last = oc == HID // 128 - 1
                vmm = mmi(psv, lh, rh, oc == 0, last, inc=last)
            psR.wrote("mm", vmm)
            T3S[par].did_read("mm", vmm)
            HQF.did_read("mm", vmm)
            FRS.write("vector")
            psR.read("vector")
            vfr = vop(lambda E, pz=psv: E.tensor_copy(fr_sb.ap()[:, 0:512], pz))
            psR.did_read("vec", vfr)
            FRS.wrote("vec", vfr)
            for tt in range(4):
                tcg = half * 4 + tt
                trv, trR = p6_tr[tt]
                trR.write("tensor")
                FRS.read("tensor")
                tmm = tri(trv, fr_sb.ap()[:, tt * 128:tt * 128 + 128], eye_f.ap(),
                          inc=True)
                trR.wrote("mm", tmm)
                FRS.did_read("mm", tmm)
                trR.read("vector")
                R2C[tt].read("vector")
                R2C[tt].write("vector")
                vr2 = vop(lambda E, tt=tt, tcg=tcg, dmc=dmc, trv=trv:
                          E.scalar_tensor_tensor(
                              r2.ap()[:, tt * D + dmc * 128:tt * D + dmc * 128 + 128],
                              trv, dq3.ap()[:, tcg:tcg + 1],
                              r2.ap()[:, tt * D + dmc * 128:tt * D + dmc * 128 + 128],
                              ALU.mult, ALU.add))
                trR.did_read("vec", vr2)
                R2C[tt].wrote("vec", vr2)

        if DEBUG_DUMP:
            S.wait("sync", "vec", S.val("vec"))
            for tz in range(4):
                v_r2d = dma("sync", "std", r2_dbg[(half * 4 + tz) * 128:(half * 4 + tz) * 128 + 128, :],
                            r2.ap()[:, tz * D:(tz + 1) * D])
            S.wait("vector", "std", v_r2d)
        if STOP_AFTER == "p6h%d" % half:
            return early_out()
        for tt in range(4):
            tcg = half * 4 + tt
            rr = r2.ap()[:, tt * D:tt * D + D]
            S.wait("vector", "vec", R2C[tt].w[1] if R2C[tt].w else 0)
            vop(lambda E, rr=rr: E.tensor_reduce(lnst.ap()[:, 0:1], rr, AX.X, ALU.add),
                inc=False)
            vdrain()
            vmean = vop(lambda E: E.tensor_scalar(
                lnst.ap()[:, 1:2], lnst.ap()[:, 0:1], -1.0 / D, None, ALU.mult))
            vdrain()
            S.wait("scalar", "vec", vmean)
            R2C[tt].read("scalar")
            vsq = aop(lambda E, rr=rr: E.activation(
                sq_scr.ap(), rr, AF.Square, accum_out=lnst.ap()[:, 2:3]))
            R2C[tt].did_read("act", vsq)
            S.wait("vector", "act", vsq)
            vop(lambda E: E.tensor_tensor(
                lnst.ap()[:, 3:4], lnst.ap()[:, 1:2], lnst.ap()[:, 1:2], ALU.mult),
                inc=False)
            vdrain()
            vvar = vop(lambda E: E.scalar_tensor_tensor(
                lnst.ap()[:, 4:5], lnst.ap()[:, 2:3], 1.0 / D, lnst.ap()[:, 3:4],
                ALU.mult, ALU.subtract))
            vdrain()
            S.wait("scalar", "vec", vvar)
            aop(lambda E: E.activation(lnst.ap()[:, 5:6], lnst.ap()[:, 4:5], AF.Ln,
                                       bias=consts.ap()[:, 0:1]), inc=False)
            vrstd = aop(lambda E: E.activation(lnst.ap()[:, 6:7], lnst.ap()[:, 5:6],
                                               AF.Exp, scale=-0.5))
            S.wait("vector", "act", vrstd)
            vop(lambda E: E.tensor_copy(spc.ap(), spc.ap()), inc=False)
            vdrain()
            vnmr = vop(lambda E: E.tensor_scalar(
                lnst.ap()[:, 7:8], lnst.ap()[:, 1:2], lnst.ap()[:, 6:7], None, ALU.mult))
            S.wait("scalar", "vec", vnmr)
            aop(lambda E: E.activation(spc.ap(), spc.ap(), AF.Identity), inc=False)
            YSB.write("scalar")
            vy = aop(lambda E, rr=rr: E.activation(
                y_sb.ap(), rr, AF.Identity,
                bias=lnst.ap()[:, 7:8], scale=lnst.ap()[:, 6:7]))
            YSB.wrote("act", vy)
            vy = aop(lambda E: E.activation(spc.ap(), spc.ap(), AF.Identity))
            YSB.wrote("act", vy)
            YSB.read("sync")
            y_store = dma("sync", "sty", y_out[tcg * 128:tcg * 128 + 128, :], y_sb.ap())
            YSB.did_read("sty", y_store)

    for sem in ("sty", "stx1", "gstt", "gso0", "gso1"):
        S.wait("sync", sem, S.val(sem))

    _emit_block()
    return nc


def kernel(L0_attn_x, L0_attn_Wq, L0_attn_bq, L0_attn_Wk, L0_attn_bk,
           L0_attn_Wv, L0_attn_bv, L0_attn_Wo, L0_attn_bo,
           L0_ffn_w1, L0_ffn_w2, L0_ffn_w3):
    x = np.ascontiguousarray(np.asarray(L0_attn_x, dtype=np.float32))
    Wq = np.asarray(L0_attn_Wq, dtype=np.float32)
    Wk = np.asarray(L0_attn_Wk, dtype=np.float32)
    Wv = np.asarray(L0_attn_Wv, dtype=np.float32)
    Wo = np.ascontiguousarray(np.asarray(L0_attn_Wo, dtype=np.float32))
    bq = np.ascontiguousarray(np.asarray(L0_attn_bq, dtype=np.float32).reshape(16, 128).T)
    bk = np.ascontiguousarray(np.asarray(L0_attn_bk, dtype=np.float32).reshape(16, 128).T)
    bv = np.asarray(L0_attn_bv, dtype=np.float32).reshape(1, D).copy()
    bo = np.asarray(L0_attn_bo, dtype=np.float32).reshape(1, D).copy()
    w1 = np.asarray(L0_ffn_w1, dtype=np.float32)
    w2 = np.asarray(L0_ffn_w2, dtype=np.float32)
    w3 = np.asarray(L0_ffn_w3, dtype=np.float32)

    def blk_w_qkv(W):
        return np.ascontiguousarray(
            W.reshape(KC, 128, NHG, 256).transpose(2, 1, 0, 3).reshape(NHG, 128, KC * 256))

    def blk_ffn(wm):
        return np.ascontiguousarray(
            wm.reshape(11, 512, KC, 128).transpose(0, 3, 2, 1).reshape(11, 128, KC * 512))

    def blk_w3(wm):
        return np.ascontiguousarray(
            wm.reshape(KC, 128, HID // 128, 128).transpose(0, 3, 2, 1)
            .reshape(KC, 128, HID))

    Wkb, Wvb = blk_w_qkv(Wk), blk_w_qkv(Wv)
    Wqb2 = np.ascontiguousarray(
        Wq.reshape(KC, 128, KC, 128).transpose(2, 1, 0, 3).reshape(KC, 128, D))
    w1b, w2b, w3b = blk_ffn(w1), blk_ffn(w2), blk_w3(w3)
    eyeb = np.eye(128, dtype=ml_dtypes.bfloat16)
    eyef = np.eye(128, dtype=np.float32)

    masks = []
    qb_ = np.arange(8)[:, None, None]
    a_ = np.arange(8)[None, None, :]
    b_ = np.arange(16)[None, :, None]
    tt_local = (qb_ * 128 + a_ * 16 + b_).reshape(TQ)
    kidx = np.arange(T)[None, :]
    for half in range(2):
        qidx = (tt_local + half * TQ)[:, None]
        m = np.where(kidx <= qidx, np.float32(0.0), np.float32(-1.0e30))
        masks.append(m.astype(ml_dtypes.bfloat16))
    eyep = np.zeros((128, 128), np.float32)
    for b2 in range(16):
        for a2 in range(8):
            eyep[b2 * 8 + a2, a2 * 16 + b2] = 1.0

    nc = build_program()

    def blk_x(xfm):
        nb = xfm.shape[1] // 256
        return np.ascontiguousarray(
            xfm.reshape(KC, 128, nb, 256).transpose(2, 1, 0, 3).reshape(nb, 128, KC * 256))

    in_maps = []
    for c in range(8):
        b, half = c // 2, c % 2
        xb = x[b]
        xq = xb[half * TQ:(half + 1) * TQ]
        hh_i = np.arange(16)[:, None]
        u_i = np.arange(64)[None, :]
        scatter = (hh_i * 128 + 64 * half + u_i).reshape(-1)
        xqs_cols = np.ascontiguousarray(xb[scatter].T)
        in_maps.append(dict(
            x_kvb=blk_x(xb.T),
            x_qb=blk_x(xqs_cols),
            x_q_tm=np.ascontiguousarray(xq),
            Wqb2=Wqb2, Wkb=Wkb, Wvb=Wvb, Wo=Wo,
            bq=bq, bk=bk, bv=bv, bo=bo,
            w1b=w1b, w2b=w2b, w3b=w3b,
            mask=masks[half], eyeb=eyeb, eyef=eyef, eyep=eyep,
        ))
    res = run_bass_kernel_spmd(nc, in_maps, core_ids=list(range(8)))
    y = np.empty((4, T, D), dtype=np.float32)
    for c in range(8):
        b, half = c // 2, c % 2
        y[b, half * TQ:(half + 1) * TQ, :] = res.results[c]["y"]
    return y


# version 06 (final)
# speedup vs baseline: 1.0597x; 1.0597x over previous
import numpy as np
import ml_dtypes
import concourse.bass as bass
import concourse.mybir as mybir
from concourse.bass_utils import run_bass_kernel_spmd

F32 = mybir.dt.float32
BF16 = mybir.dt.bfloat16
I32 = mybir.dt.int32
AF = mybir.ActivationFunctionType
ALU = mybir.AluOpType
AX = mybir.AxisListType

D = 2048
T = 2048
TQ = 1024
HID = 5632
KC = 16
HG = 2
NHG = 8
SCL = 0.08838834764831845
W_N = float(HID * D)
P1T = 1024
NTM = 88
KB = 1024
DEBUG_DUMP = False
STOP_AFTER = None

ENG_SEM = {"tensor": "mm", "vector": "vec", "scalar": "act"}


class Sched:
    def __init__(self, nc):
        self.nc = nc
        self.items = {e: [] for e in ("sync", "tensor", "vector", "scalar", "gpsimd")}
        self.sems = {}
        self.counts = {}
        for s in ("mm", "vec", "act"):
            self.sems[s] = nc.alloc_semaphore(s)
            self.counts[s] = 0
        self.nwaits = 0

    def get_sem(self, name):
        if name not in self.sems:
            self.sems[name] = self.nc.alloc_semaphore(name)
            self.counts[name] = 0
        return self.sems[name]

    def emit(self, eng, fn):
        self.items[eng].append(fn)

    def inc(self, name, n=1):
        self.get_sem(name)
        self.counts[name] += n
        return self.sems[name], self.counts[name]

    def val(self, name):
        return self.counts[name]

    def wait(self, eng, name, v):
        if v <= 0:
            return
        if ENG_SEM.get(eng) == name:
            return
        self.nwaits += 1
        s = self.get_sem(name)
        self.emit(eng, lambda E, s=s, v=v: E.wait_ge(s, v))


class Res:

    def __init__(self, S):
        self.S = S
        self.w = None
        self.r = {}

    def write(self, eng):
        if self.w:
            self.S.wait(eng, *self.w)
        for sn, v in self.r.items():
            self.S.wait(eng, sn, v)

    def wrote(self, semname, val):
        self.w = (semname, val)
        self.r = {}

    def read(self, eng):
        if self.w:
            self.S.wait(eng, *self.w)

    def did_read(self, semname, val):
        self.r[semname] = max(self.r.get(semname, 0), val)


def build_program():
    nc = bass.Bass("TRN2", target_bir_lowering=False, detect_race_conditions=False)

    din = lambda n, s, dt=F32: nc.dram_tensor(n, s, dt, kind="ExternalInput").ap()
    x_kvb = din("x_kvb", [8, 128, KC * 256])
    x_qb = din("x_qb", [4, 128, KC * 256])
    x_q_tm = din("x_q_tm", [TQ, D])
    Wqb2 = din("Wqb2", [KC, 128, D])
    Wkb = din("Wkb", [NHG, 128, KC * 256])
    Wvb = din("Wvb", [NHG, 128, KC * 256])
    Wod = din("Wo", [D, D])
    bqd, bkd = din("bq", [128, 16]), din("bk", [128, 16])
    bvd, bod = din("bv", [1, D]), din("bo", [1, D])
    w1bd = din("w1b", [11, 128, KC * 512])
    w2bd = din("w2b", [11, 128, KC * 512])
    w3bd = din("w3b", [KC, 128, (HID // 128) * 128])
    maskd = din("mask", [TQ, T], BF16)
    eyebd = din("eyeb", [128, 128], BF16)
    eyefd = din("eyef", [128, 128], F32)
    eyepd = din("eyep", [128, 128], F32)
    y_out = nc.dram_tensor("y", [TQ, D], F32, kind="ExternalOutput").ap()

    _scrkind = dict(kind="ExternalOutput") if DEBUG_DUMP else {}
    out_scr = nc.dram_tensor("out_scr", [D, TQ], F32, **_scrkind).ap()
    q_scr = nc.dram_tensor("q_scr", [16, 128, 1024], F32).ap()
    x1_scr = nc.dram_tensor("x1_scr", [TQ, D], F32, **_scrkind).ap()
    t1_scr = nc.dram_tensor("t1_scr", [D * HID], BF16, **_scrkind).ap()
    t2_scr = nc.dram_tensor("t2_scr", [D * HID], BF16, **_scrkind).ap()
    t3_scr = nc.dram_tensor("t3_scr", [KC * HID * 128], BF16, **_scrkind).ap()
    h_dbg = nc.dram_tensor("h_dbg", [128, 2 * HID], F32, kind="ExternalOutput").ap() if DEBUG_DUMP else None
    r2_dbg = nc.dram_tensor("r2_dbg", [TQ, D], F32, kind="ExternalOutput").ap() if DEBUG_DUMP else None
    hq_dbg = nc.dram_tensor("hq_dbg", [128, (HID // 128) * 512], BF16, kind="ExternalOutput").ap() if DEBUG_DUMP else None
    ha_dbg = nc.dram_tensor("ha_dbg", [128, 8], F32, kind="ExternalOutput").ap() if DEBUG_DUMP else None
    h2_dbg = nc.dram_tensor("h2_dbg", [128, 2 * HID], F32, kind="ExternalOutput").ap() if DEBUG_DUMP else None
    xq_dbg = nc.dram_tensor("xq_dbg", [128, KC * TQ], BF16, kind="ExternalOutput").ap() if DEBUG_DUMP else None
    ln2_dbg = nc.dram_tensor("ln2_dbg", [128, 8 * 16], F32, kind="ExternalOutput").ap() if DEBUG_DUMP else None
    q_dbg = nc.dram_tensor("q_dbg", [128, 4 * 512], F32, kind="ExternalOutput").ap() if DEBUG_DUMP else None
    r1_dbg = nc.dram_tensor("r1_dbg", [TQ, D], F32, kind="ExternalOutput").ap() if DEBUG_DUMP else None
    bo_dbg = nc.dram_tensor("bo_dbg", [128, D], F32, kind="ExternalOutput").ap() if DEBUG_DUMP else None
    ln_dbg = nc.dram_tensor("ln_dbg", [128, 8 * 32], F32, kind="ExternalOutput").ap() if DEBUG_DUMP else None

    sb = nc.alloc_sbuf_tensor
    consts = sb("consts", [128, 8], F32)
    ones_row = sb("ones_row", [1, 128], F32)
    eye_b = sb("eye_b", [128, 128], BF16)
    eye_f = sb("eye_f", [128, 128], F32)
    eye_p = sb("eye_p", [128, 128], F32)
    bqc = sb("bqc", [128, 16], F32)
    bkc = sb("bkc", [128, 16], F32)
    bv_row = sb("bv_row", [1, 256], F32)
    bo_row = sb("bo_row", [1, 512], F32)
    bvbc = sb("bvbc", [128, 256], F32)
    p1a = sb("p1a", [128, 3 * NTM], F32)
    wmean = sb("wmean", [128, 16], F32)
    stt_ = sb("stt_", [128, 16], F32)
    ws = sb("ws", [128, 2 * P1T], F32)
    tt_b = sb("tt_b", [128, P1T], BF16)
    ta_b = sb("ta_b", [128, P1T], BF16)
    qy = sb("qy", [128, 512], F32)
    qi = sb("qi", [128, 512], I32)
    qd = sb("qd", [128, 512], F32)
    qa = sb("qa", [128, 512], F32)
    qb = sb("qb", [128, 512], F32)
    qk = sb("qk", [128, 512], BF16)
    sxa = sb("sxa", [128, 8], F32)
    dq1 = sb("dq1", [128, 8], F32)
    dq2 = sb("dq2", [128, 8], F32)
    dq3 = sb("dq3", [128, 8], F32)
    hamax = sb("hamax", [128, 8], F32)
    lnst = sb("lnst", [128, 16], F32)
    spc = sb("spc", [128, 512], F32)

    base = int(nc.bytes_allocated(bass.MemorySpace.SBUF))
    base = (base + 127) & ~127

    def at(name, shape, dt, off):
        return nc.alloc_sbuf_tensor_at(name, shape, dt, offset=base + off)

    xs = at("xs", [128, 2 * KC * 256], F32, 0)
    k_fm = at("k_fm", [128, HG * T], F32, 32 * KB)
    v_tm = at("v_tm", [128, 16 * HG * 128], F32, 48 * KB)
    q_h = at("q_h", [128, 2 * 1024], F32, 64 * KB)
    wq_s = at("wq_s", [128, KC * 256], F32, 72 * KB)
    wk_s = at("wk_s", [128, KC * 256], F32, 88 * KB)
    wv_s = at("wv_s", [128, KC * 256], F32, 104 * KB)
    e_sb = at("e_sb", [128, 4 * T], F32, 120 * KB)
    eT_sb = at("eT_sb", [128, 2 * 256], F32, 152 * KB)
    msk_s = at("msk_s", [128, 2 * T], BF16, 154 * KB)
    o_sb = at("o_sb", [128, 2 * 256], F32, 162 * KB)
    qgb = at("qgb", [128, 2 * 128], F32, 164 * KB)

    r1 = at("r1", [128, 8 * D], F32, 0)
    wo_s = at("wo_s", [128, 2 * 512], F32, 64 * KB)
    os_s = at("os_s", [128, 2 * 512], F32, 68 * KB)
    o3_sb = at("o3_sb", [128, 512], F32, 72 * KB)
    xqs = at("xqs", [128, 4 * D], F32, 74 * KB)
    bo_bc = at("bo_bc", [128, D], F32, 106 * KB)
    sq_scr = at("sq_scr", [128, D], F32, 114 * KB)
    x1_sb = at("x1_sb", [128, D], F32, 122 * KB)
    xq_fm = at("xq_fm", [128, KC * TQ], BF16, 132 * KB)

    h_q = at("h_q", [128, 2 * HID], F32, 0)
    t_s = at("t_s", [128, 2 * KC * 512], BF16, 44 * KB)
    hq_fm = at("hq_fm", [128, (HID // 128) * 512], BF16, 76 * KB)
    hs1 = at("hs1", [128, 2 * 512], F32, 120 * KB)
    hs2 = at("hs2", [128, 2 * 512], F32, 124 * KB)
    t3_sb = at("t3_sb", [128, 2 * HID], BF16, 0)
    r2 = at("r2", [128, 4 * D], F32, 22 * KB)
    fr_sb = at("fr_sb", [128, 1024], F32, 54 * KB)
    y_sb = at("y_sb", [128, D], F32, 58 * KB)

    ps_big_t = nc.alloc_psum_tensor("ps_big", [128, 2048], F32)
    ps_mid_t = nc.alloc_psum_tensor("ps_mid", [128, 1024], F32)
    ps_a_t = nc.alloc_psum_tensor("ps_a", [128, 512], F32)
    ps_b_t = nc.alloc_psum_tensor("ps_b", [128, 512], F32)

    S = Sched(nc)

    def dma(eng, sem_name, out_ap, in_ap):
        S.get_sem(sem_name)
        s, v = S.inc(sem_name, 16)
        S.emit(eng, lambda E, o=out_ap, i=in_ap, s=s: E.dma_start(out=o, in_=i).then_inc(s, 16))
        return v

    def mmi(out_ap, lh, rh, start, stop, inc=False):
        if inc:
            s, v = S.inc("mm")
            S.emit("tensor", lambda E, o=out_ap, l=lh, r=rh, a=start, b=stop, s=s:
                   E.matmul(o, l, r, start=a, stop=b).then_inc(s, 1))
            return v
        S.emit("tensor", lambda E, o=out_ap, l=lh, r=rh, a=start, b=stop:
               E.matmul(o, l, r, start=a, stop=b))
        return None

    def tri(out_ap, in_ap, ident, inc=False):
        if inc:
            s, v = S.inc("mm")
            S.emit("tensor", lambda E, o=out_ap, i=in_ap, d=ident, s=s:
                   E.transpose(o, i, d).then_inc(s, 1))
            return v
        S.emit("tensor", lambda E, o=out_ap, i=in_ap, d=ident: E.transpose(o, i, d))
        return None

    def vop(fn, inc=True):
        if inc:
            s, v = S.inc("vec")
            S.emit("vector", lambda E, f=fn, s=s: f(E).then_inc(s, 1))
            return v
        S.emit("vector", lambda E, f=fn: f(E))
        return None

    def aop(fn, inc=True):
        if inc:
            s, v = S.inc("act")
            S.emit("scalar", lambda E, f=fn, s=s: f(E).then_inc(s, 1))
            return v
        S.emit("scalar", lambda E, f=fn: f(E))
        return None

    def vdrain():
        S.emit("vector", lambda E: E.drain())

    def barrier():
        vs = {"mm": S.val("mm")}
        for eng, sem in (("vector", "vec"), ("scalar", "act")):
            s, v = S.inc(sem)
            S.emit(eng, lambda E, s=s: E.nop(nofuse=True).then_inc(s, 1))
            vs[sem] = v
        for sem in list(S.counts):
            if sem not in vs:
                vs[sem] = S.val(sem)
        for eng in ("sync", "tensor", "vector", "scalar", "gpsimd"):
            for sem, v in vs.items():
                S.wait(eng, sem, v)

    def _emit_block():
        with nc.Block() as block:
            @block.sync
            def _(E):
                for f in S.items["sync"]:
                    f(E)

            @block.tensor
            def _(E):
                for f in S.items["tensor"]:
                    f(E)

            @block.vector
            def _(E):
                for f in S.items["vector"]:
                    f(E)

            @block.scalar
            def _(E):
                for f in S.items["scalar"]:
                    f(E)

            @block.gpsimd
            def _(E):
                for f in S.items["gpsimd"]:
                    f(E)

    PS_big = Res(S)
    PS_mid0 = Res(S)
    PS_mid1 = Res(S)
    PS_a = Res(S)
    PS_b = Res(S)
    PS_big2 = Res(S)
    ps_big = ps_big_t.ap()
    ps_mid = ps_mid_t.ap()
    ps_a = ps_a_t.ap()
    ps_b = ps_b_t.ap()

    dma("sync", "ldp", eye_b.ap(), eyebd)
    v_eye = dma("sync", "ldp", eye_f.ap(), eyefd)
    v_eye = dma("sync", "ldp", eye_p.ap(), eyepd)
    dma("sync", "ldp", bqc.ap(), bqd)
    dma("sync", "ldp", bkc.ap(), bkd)
    vop(lambda E: E.memset(consts.ap()[:, 0:1], 1e-5), inc=False)
    vop(lambda E: E.memset(consts.ap()[:, 1:2], 1.0), inc=False)
    vop(lambda E: E.memset(spc.ap(), 1.0), inc=False)
    vop(lambda E: E.memset(lnst.ap(), 0.0), inc=False)
    vop(lambda E: E.memset(stt_.ap(), 0.0), inc=False)
    v_consts = vop(lambda E: E.memset(ones_row.ap(), 1.0))
    for _eng in ("tensor", "vector", "scalar"):
        S.wait(_eng, "ldp", S.val("ldp"))
    S.wait("tensor", "vec", v_consts)
    S.wait("scalar", "vec", v_consts)

    w1f = w1bd.rearrange("a b c -> (a b c)").rearrange("(n p m) -> n p m", p=128, m=P1T)
    w2f = w2bd.rearrange("a b c -> (a b c)").rearrange("(n p m) -> n p m", p=128, m=P1T)
    w3f = w3bd.rearrange("a b c -> (a b c)").rearrange("(n p m) -> n p m", p=128, m=P1T)
    t1f = t1_scr.rearrange("(n p m) -> n p m", p=128, m=P1T)
    t2f = t2_scr.rearrange("(n p m) -> n p m", p=128, m=P1T)
    t3f = t3_scr.rearrange("(n p m) -> n p m", p=128, m=P1T)
    wsrc = [w1f, w2f, w3f]
    tdst = [t1f, t2f, t3f]

    bg_state = dict(tile_act={}, pass2_store={}, tern_done={}, fin_evac=0)

    def bg_steps():
        tile_act = bg_state["tile_act"]
        pass2_store = bg_state["pass2_store"]
        for m in range(3):
            for tn in range(NTM):
                idx = m * NTM + tn
                par = idx % 2
                buf = ws.ap()[:, par * P1T:(par + 1) * P1T]

                def step(m=m, tn=tn, idx=idx, buf=buf, par=par):
                    if idx >= 2:
                        S.wait("gpsimd", "act", tile_act[idx - 2])
                    vldw = dma("gpsimd", "gld%d" % par, buf, wsrc[m][tn])
                    S.wait("scalar", "gld%d" % par, vldw)
                    tile_act[idx] = aop(lambda E, b=buf, c=idx: E.activation(
                        ta_b.ap()[:, 0:P1T], b, AF.Abs,
                        accum_out=p1a.ap()[:, c:c + 1]))
                yield step
        for m in range(3):
            def fin(m=m):
                S.wait("vector", "act", bg_state["tile_act"][(m + 1) * NTM - 1])
                v1 = vop(lambda E, m=m: E.tensor_reduce(
                    stt_.ap()[:, 8:9], p1a.ap()[:, m * NTM:(m + 1) * NTM], AX.X, ALU.add))
                PS_a.write("tensor")
                S.wait("tensor", "vec", v1)
                mt = mmi(ps_a[0:1, 0:1], stt_.ap()[:, 8:9], consts.ap()[:, 1:2],
                         True, True, inc=True)
                PS_a.wrote("mm", mt)
                PS_a.read("vector")
                v2 = vop(lambda E: E.tensor_copy(stt_.ap()[0:1, 9:10], ps_a[0:1, 0:1]))
                PS_a.did_read("vec", v2)
                PS_a.write("tensor")
                S.wait("tensor", "vec", v2)
                mt2 = mmi(ps_a[:, 1:2], ones_row.ap()[0:1, 0:128], stt_.ap()[0:1, 9:10],
                          True, True, inc=True)
                PS_a.wrote("mm", mt2)
                PS_a.read("vector")
                v3 = vop(lambda E, m=m: E.tensor_scalar(
                    wmean.ap()[:, m:m + 1], ps_a[:, 1:2], 1.0 / W_N, 1e-5,
                    ALU.mult, ALU.max))
                PS_a.did_read("vec", v3)
                S.wait("scalar", "vec", v3)
                aop(lambda E, m=m: E.activation(
                    wmean.ap()[:, 3 + m:4 + m], wmean.ap()[:, m:m + 1], AF.Identity,
                    scale=0.5), inc=False)
                v_th = aop(lambda E, m=m: E.activation(
                    wmean.ap()[:, 6 + m:7 + m], wmean.ap()[:, m:m + 1], AF.Identity,
                    scale=-0.5))
                S.wait("vector", "act", v_th)
            yield fin
            for tn in range(NTM):
                par = tn % 2
                buf = ws.ap()[:, par * P1T:(par + 1) * P1T]

                def tern(m=m, tn=tn, buf=buf, par=tn % 2):
                    pass2_store = bg_state["pass2_store"]
                    if m == 0 and tn < 2:
                        S.wait("gpsimd", "act", bg_state["tile_act"][3 * NTM - 1])
                    else:
                        prev = pass2_store.get((m, tn - 2), pass2_store.get((m - 1, NTM + tn - 2)))
                        if prev:
                            S.wait("gpsimd", "gstt", prev)
                    vldw = dma("gpsimd", "gld%d" % par, buf, wsrc[m][tn])
                    S.wait("vector", "gld%d" % par, vldw)
                    vop(lambda E, b=buf, m=m: E.tensor_scalar(
                        ta_b.ap()[:, 0:P1T], b, wmean.ap()[:, 3 + m:4 + m], None,
                        ALU.is_ge), inc=False)
                    vt = vop(lambda E, b=buf, m=m: E.scalar_tensor_tensor(
                        tt_b.ap()[:, 0:P1T], b, wmean.ap()[:, 6 + m:7 + m],
                        ta_b.ap()[:, 0:P1T], ALU.is_le, ALU.subtract))
                    S.wait("gpsimd", "vec", vt)
                    pass2_store[(m, tn)] = dma("gpsimd", "gstt", tdst[m][tn],
                                               tt_b.ap()[:, 0:P1T])
                    if tn == NTM - 1:
                        bg_state["tern_done"][m] = pass2_store[(m, tn)]
                yield tern

    bg_iter = bg_steps()
    bg_empty = [False]

    def drain_bg(k):
        for _ in range(k):
            stp = next(bg_iter, None)
            if stp is None:
                bg_empty[0] = True
                return
            stp()

    def drain_all_bg():
        while not bg_empty[0]:
            drain_bg(1)


    def xs_view(par):
        return xs.ap()[:, par * KC * 256:(par + 1) * KC * 256]

    KFM = Res(S)
    VTM = Res(S)
    QH = [Res(S), Res(S)]
    QGB = [Res(S), Res(S)]
    WSL = Res(S)
    XSB = [Res(S), Res(S)]
    ESB = [Res(S) for _ in range(4)]
    ETS = [Res(S), Res(S)]
    OSB = [Res(S), Res(S)]
    MSK = Res(S)
    BVB = Res(S)
    BVR = Res(S)
    BOR = Res(S)

    out_store_final = 0

    prev_head = [None]

    def emit_pv(ph):
        hg, h, j, nk, eoa, eob, gh, vnorm, opar = ph
        pv_stop = 0
        last_tr = 0
        for kc2 in range(nk):
            tpar = kc2 % 2
            slot = ps_a if tpar == 0 else ps_b
            slotR = PS_a if tpar == 0 else PS_b
            slotR.write("tensor")
            S.wait("tensor", "vec", vnorm)
            tri(slot[:, 0:128], e_sb.ap()[:, eoa + kc2 * 128:eoa + kc2 * 128 + 128],
                eye_p.ap())
            t_mm = tri(slot[:, 128:256], e_sb.ap()[:, eob + kc2 * 128:eob + kc2 * 128 + 128],
                       eye_p.ap(), inc=True)
            slotR.wrote("mm", t_mm)
            slotR.read("vector")
            ETS[tpar].write("vector")
            ev = vop(lambda E, tp=tpar, sl=slot: E.tensor_copy(
                eT_sb.ap()[:, tp * 256:tp * 256 + 256], sl[:, 0:256]))
            slotR.did_read("vec", ev)
            ETS[tpar].wrote("vec", ev)
            ETS[tpar].read("tensor")
            if kc2 == 0:
                PS_mid1.write("tensor")
            vt = v_tm.ap()[:, kc2 * 256 + h * 128:kc2 * 256 + h * 128 + 128]
            pv = mmi(ps_mid[:, 512:512 + 256], vt,
                     eT_sb.ap()[:, tpar * 256:tpar * 256 + 256],
                     kc2 == 0, kc2 == nk - 1, inc=(kc2 == nk - 1))
            ETS[tpar].did_read("mm", pv if pv else S.val("mm"))
            if pv:
                pv_stop = pv
        PS_mid1.wrote("mm", pv_stop)
        VTM.did_read("mm", pv_stop)
        ESB[(h % 2) * 2].did_read("mm", pv_stop)
        ESB[(h % 2) * 2 + 1].did_read("mm", pv_stop)
        PS_mid1.read("vector")
        OSB[opar].write("vector")
        ve = vop(lambda E, op=opar: E.tensor_copy(
            o_sb.ap()[:, op * 256:op * 256 + 256], ps_mid[:, 512:512 + 256]))
        PS_mid1.did_read("vec", ve)
        OSB[opar].wrote("vec", ve)
        OSB[opar].read("gpsimd")
        vst = dma("gpsimd", "gso%d" % opar,
                  out_scr[gh * 128:(gh + 1) * 128, 2 * j * 128:2 * j * 128 + 256],
                  o_sb.ap()[:, opar * 256:opar * 256 + 256])
        OSB[opar].did_read("gso%d" % opar, vst)
        return pv_stop, vst

    opar_ctr = 0
    drain_bg(40)

    WQ2 = [Res(S), Res(S)]
    QEV = Res(S)
    q_ev = bvbc
    q_store_final = 0
    for xqb in range(4):
        par = xqb % 2
        XSB[par].write("sync")
        v_x = dma("sync", "ldx%d" % par, xs_view(par), x_qb[xqb])
        XSB[par].wrote("ldx%d" % par, v_x)
        for oc in range(KC):
            wpar = oc % 2
            WQ2[wpar].write("sync")
            v_w = dma("sync", "ldwq%d" % wpar,
                      wq_s.ap()[:, wpar * 8 * KB // 4:wpar * 8 * KB // 4 + D]
                      if False else wq_s.ap()[:, wpar * D:wpar * D + D], Wqb2[oc])
            WQ2[wpar].wrote("ldwq%d" % wpar, v_w)
            psR = PS_a if (oc % 2 == 0) else PS_b
            ps = ps_a if (oc % 2 == 0) else ps_b
            psR.write("tensor")
            XSB[par].read("tensor")
            WQ2[wpar].read("tensor")
            for kc in range(KC):
                lh = wq_s.ap()[:, wpar * D + kc * 128:wpar * D + kc * 128 + 128]
                rh = xs_view(par)[:, kc * 256:kc * 256 + 256]
                last = kc == KC - 1
                vmm = mmi(ps[:, 0:256], lh, rh, kc == 0, last, inc=last)
            psR.wrote("mm", vmm)
            XSB[par].did_read("mm", vmm)
            WQ2[wpar].did_read("mm", vmm)
            QEV.write("vector")
            psR.read("vector")
            vq = vop(lambda E, pz=ps, oc=oc: E.tensor_scalar(
                q_ev.ap(), pz[:, 0:256], bqc.ap()[:, oc:oc + 1], None, ALU.add))
            psR.did_read("vec", vq)
            QEV.wrote("vec", vq)
            QEV.read("gpsimd")
            for hh_l in range(4):
                gh2 = xqb * 4 + hh_l
                q_store_final = dma("gpsimd", "gsq",
                                    q_scr[gh2, :, oc * 64:oc * 64 + 64],
                                    q_ev.ap()[:, hh_l * 64:hh_l * 64 + 64])
            QEV.did_read("gsq", q_store_final)

    for hg in range(NHG):
        drain_bg(2)
        WSL.write("sync")
        sl = hg * 256
        dma("sync", "ldw", wk_s.ap(), Wkb[hg])
        v_wsl = dma("sync", "ldw", wv_s.ap(), Wvb[hg])
        WSL.wrote("ldw", v_wsl)
        BVR.write("sync")
        v_bvr = dma("sync", "ldbv", bv_row.ap(), bvd[0:1, sl:sl + 256])
        BVR.wrote("ldbv", v_bvr)
        PS_mid0.write("tensor")
        WSL.read("tensor")
        BVR.read("tensor")
        S.wait("tensor", "vec", v_consts)
        mm_bv = mmi(ps_mid[:, 0:256], ones_row.ap()[0:1, 0:128],
                    bv_row.ap()[0:1, 0:256], True, True, inc=True)
        BVR.did_read("mm", mm_bv)
        PS_mid0.wrote("mm", mm_bv)
        BVB.write("vector")
        PS_mid0.read("vector")
        v_bvbc = vop(lambda E: E.tensor_copy(bvbc.ap(), ps_mid[:, 0:256]))
        PS_mid0.did_read("vec", v_bvbc)
        BVB.wrote("vec", v_bvbc)

        kv_evac = 0
        for tb in range(8):
            par = tb % 2
            XSB[par].write("sync")
            if tb < 8:
                v_x = dma("sync", "ldx%d" % par, xs_view(par), x_kvb[tb])
            else:
                v_x = dma("sync", "ldx%d" % par, xs_view(par), x_qb[tb - 8])
            XSB[par].wrote("ldx%d" % par, v_x)
            if tb < 8:
                for oc in range(2):
                    psR = PS_mid0 if oc == 0 else PS_mid1
                    ps = ps_mid[:, 0:256] if oc == 0 else ps_mid[:, 512:512 + 256]
                    psR.write("tensor")
                    XSB[par].read("tensor")
                    WSL.read("tensor")
                    for kc in range(KC):
                        lh = wk_s.ap()[:, kc * 256 + oc * 128:kc * 256 + oc * 128 + 128]
                        rh = xs_view(par)[:, kc * 256:kc * 256 + 256]
                        last = kc == KC - 1
                        vmm = mmi(ps, lh, rh, kc == 0, last, inc=last)
                    psR.wrote("mm", vmm)
                    XSB[par].did_read("mm", vmm)
                    WSL.did_read("mm", vmm)
                    gc = hg * 2 + oc
                    KFM.write("vector")
                    psR.read("vector")
                    kv_evac = vop(lambda E, pz=ps, oc=oc, tb=tb, gc=gc: E.tensor_scalar(
                        k_fm.ap()[:, oc * T + tb * 256:oc * T + tb * 256 + 256],
                        pz, bkc.ap()[:, gc:gc + 1], None, ALU.add))
                    psR.did_read("vec", kv_evac)
                    KFM.wrote("vec", kv_evac)
                for tc in range(2):
                    psR = PS_a if tc == 0 else PS_b
                    ps = ps_a if tc == 0 else ps_b
                    psR.write("tensor")
                    XSB[par].read("tensor")
                    WSL.read("tensor")
                    for kc in range(KC):
                        lh = xs_view(par)[:, kc * 256 + tc * 128:kc * 256 + tc * 128 + 128]
                        rh = wv_s.ap()[:, kc * 256:kc * 256 + 256]
                        last = kc == KC - 1
                        vmm = mmi(ps[:, 0:256], lh, rh, kc == 0, last, inc=last)
                    psR.wrote("mm", vmm)
                    XSB[par].did_read("mm", vmm)
                    WSL.did_read("mm", vmm)
                    VTM.write("vector")
                    BVB.read("vector")
                    psR.read("vector")
                    kv_evac = vop(lambda E, pz=ps, tb=tb, tc=tc: E.tensor_tensor(
                        v_tm.ap()[:, (tb * 2 + tc) * 256:(tb * 2 + tc) * 256 + 256],
                        pz[:, 0:256], bvbc.ap(), ALU.add))
                    psR.did_read("vec", kv_evac)
                    VTM.wrote("vec", kv_evac)
            drain_bg(2)
        v_p1_done = kv_evac

        for j in range(4):
            L = 1024 + 256 * (j + 1)
            nk = L // 128
            MSK.write("sync")
            dma("sync", "ldm", msk_s.ap()[:, 0:L],
                maskd[2 * j * 128:2 * j * 128 + 128, 0:L])
            v_msk = dma("sync", "ldm", msk_s.ap()[:, T:T + L],
                        maskd[(2 * j + 1) * 128:(2 * j + 1) * 128 + 128, 0:L])
            MSK.wrote("ldm", v_msk)
            for h in range(HG):
                hpar = h % 2
                gh = hg * HG + h
                vnorms = [0, 0]
                QH[hpar].write("sync")
                if hg == 0 and j == 0 and h == 0:
                    S.wait("sync", "gsq", q_store_final)
                v_qh = dma("sync", "ldq%d" % hpar,
                           q_h.ap()[:, hpar * 1024:hpar * 1024 + 1024], q_scr[gh])
                QH[hpar].wrote("ldq%d" % hpar, v_qh)
                for qq in range(2):
                    qb_ = 2 * j + qq
                    eoff = hpar * 2 * T + qq * T
                    eres = ESB[hpar * 2 + qq]
                    PS_big.write("tensor")
                    MSK.read("tensor")
                    KFM.read("tensor")
                    S.wait("tensor", "vec", v_p1_done)
                    nsl = (L + 511) // 512
                    qg3 = q_h.ap()[:, hpar * 1024:hpar * 1024 + 1024] \
                        .rearrange("p (c u) -> p c u", u=64)[:, :, qb_ * 8:qb_ * 8 + 8]
                    QGB[qq].write("vector")
                    QH[hpar].read("vector")
                    v_qg = vop(lambda E, qg3=qg3, qq=qq: E.tensor_copy(
                        qgb.ap()[:, qq * 128:qq * 128 + 128]
                        .rearrange("p (c u) -> p c u", u=8), qg3))
                    QH[hpar].did_read("vec", v_qg)
                    QGB[qq].wrote("vec", v_qg)
                    QGB[qq].read("tensor")
                    for ksl in range(nsl):
                        w = min(512, L - ksl * 512)
                        lh = qgb.ap()[:, qq * 128:qq * 128 + 128]
                        rh = k_fm.ap()[:, h * T + ksl * 512:h * T + ksl * 512 + w]
                        mmi(ps_big[:, ksl * 512:ksl * 512 + w], lh, rh, True, False)
                        mrh = msk_s.ap()[:, qq * T + ksl * 512:qq * T + ksl * 512 + w]
                        sc_mm = mmi(ps_big[:, ksl * 512:ksl * 512 + w], eye_b.ap(), mrh,
                                    False, True, inc=(ksl == nsl - 1))
                    PS_big.wrote("mm", sc_mm)
                    KFM.did_read("mm", sc_mm)
                    QGB[qq].did_read("mm", sc_mm)
                    MSK.did_read("mm", sc_mm)
                    PS_big.read("vector")
                    vmax = vop(lambda E, L=L: E.tensor_reduce(
                        stt_.ap()[:, 0:1], ps_big[:, 0:L], AX.X, ALU.max))
                    PS_big.did_read("vec", vmax)
                    vdrain()
                    vbias = vop(lambda E: E.tensor_scalar(
                        stt_.ap()[:, 1:2], stt_.ap()[:, 0:1], -SCL, None, ALU.mult))
                    S.wait("scalar", "vec", vbias)
                    eres.write("scalar")
                    PS_big.read("scalar")
                    vexp = aop(lambda E, eo=eoff, L=L: E.activation(
                        e_sb.ap()[:, eo:eo + L], ps_big[:, 0:L], AF.Exp,
                        bias=stt_.ap()[:, 1:2], scale=SCL,
                        accum_out=stt_.ap()[:, 2:3]))
                    PS_big.did_read("act", vexp)
                    aop(lambda E: E.activation(stt_.ap()[:, 3:4], stt_.ap()[:, 2:3],
                                               AF.Ln), inc=False)
                    vrs = aop(lambda E: E.activation(stt_.ap()[:, 4:5], stt_.ap()[:, 3:4],
                                                     AF.Exp, scale=-1.0))
                    S.wait("vector", "act", vrs)
                    vnorms[qq] = vop(lambda E, eo=eoff, L=L: E.tensor_scalar(
                        e_sb.ap()[:, eo:eo + L], e_sb.ap()[:, eo:eo + L],
                        stt_.ap()[:, 4:5], None, ALU.mult))
                    eres.wrote("vec", vnorms[qq])
                    if qq == 0 and prev_head[0] is not None:
                        _, out_store_final = emit_pv(prev_head[0])
                        prev_head[0] = None
                if prev_head[0] is not None:
                    _, out_store_final = emit_pv(prev_head[0])
                prev_head[0] = (hg, h, j, nk, hpar * 2 * T, hpar * 2 * T + T, gh,
                                vnorms[1], opar_ctr % 2)
                opar_ctr += 1
                drain_bg(3)
        if prev_head[0] is not None:
            _, out_store_final = emit_pv(prev_head[0])
            prev_head[0] = None
    def early_out():
        vz = vop(lambda E: E.memset(x1_sb.ap(), 0.5))
        S.wait("sync", "vec", vz)
        vst = dma("sync", "sty", y_out[0:128, :], x1_sb.ap())
        for sem in list(S.counts):
            if sem.startswith(("sty", "stx1", "gstt", "gso")):
                S.wait("sync", sem, S.val(sem))
        _emit_block()
        return nc

    if STOP_AFTER == "attn_nobg":
        return early_out()
    drain_all_bg()
    if STOP_AFTER in ("pass", "attn"):
        return early_out()

    barrier()
    WOS = [Res(S), Res(S)]
    OSS = [Res(S), Res(S)]
    O3 = Res(S)
    XQS = [Res(S) for _ in range(4)]
    R1C = [Res(S) for _ in range(8)]
    X1SB = Res(S)
    QKR = Res(S)
    XQFM = Res(S)

    for s4 in range(4):
        BOR.write("sync")
        v_bor = dma("sync", "ldbo", bo_row.ap(), bod[0:1, s4 * 512:s4 * 512 + 512])
        BOR.wrote("ldbo", v_bor)
        PS_mid0.write("tensor")
        BOR.read("tensor")
        mm_bo = mmi(ps_mid[:, 0:512], ones_row.ap()[0:1, 0:128],
                    bo_row.ap()[0:1, 0:512], True, True, inc=True)
        BOR.did_read("mm", mm_bo)
        PS_mid0.wrote("mm", mm_bo)
        PS_mid0.read("vector")
        v_bo_bc = vop(lambda E, s4=s4: E.tensor_copy(
            bo_bc.ap()[:, s4 * 512:s4 * 512 + 512], ps_mid[:, 0:512]))
        PS_mid0.did_read("vec", v_bo_bc)
    if DEBUG_DUMP:
        S.wait("sync", "vec", v_bo_bc)
        dma("sync", "std", bo_dbg, bo_bc.ap())

    p3_ps = [ps_mid[:, 0:512], ps_mid[:, 512:1024], ps_a[:, 0:512], ps_b[:, 0:512]]
    p3_R = [PS_mid0, PS_mid1, PS_a, PS_b]
    S.wait("sync", "gso0", S.val("gso0"))
    S.wait("sync", "gso1", S.val("gso1"))

    for tokb in range(2):
        for tt in range(4):
            tc = tokb * 4 + tt
            XQS[tt].write("sync")
            v_xq = dma("sync", "ldxq", xqs.ap()[:, tt * D:tt * D + D],
                       x_q_tm[tc * 128:tc * 128 + 128, :])
            XQS[tt].wrote("ldxq", v_xq)
        for dmg in range(4):
            for dmc in range(4):
                p3_R[dmc].write("tensor")
            for kc in range(KC):
                par = kc % 2
                WOS[par].write("sync")
                v_wo = dma("sync", "ldwo%d" % par, wo_s.ap()[:, par * 512:par * 512 + 512],
                           Wod[kc * 128:kc * 128 + 128, dmg * 512:dmg * 512 + 512])
                WOS[par].wrote("ldwo%d" % par, v_wo)
                OSS[par].write("sync")
                v_os = dma("sync", "ldos%d" % par, os_s.ap()[:, par * 512:par * 512 + 512],
                           out_scr[kc * 128:kc * 128 + 128, tokb * 512:tokb * 512 + 512])
                OSS[par].wrote("ldos%d" % par, v_os)
                WOS[par].read("tensor")
                OSS[par].read("tensor")
                for dmc in range(4):
                    lh = wo_s.ap()[:, par * 512 + dmc * 128:par * 512 + dmc * 128 + 128]
                    last = kc == KC - 1
                    gmm = mmi(p3_ps[dmc], lh, os_s.ap()[:, par * 512:par * 512 + 512],
                              kc == 0, last, inc=(dmc == 3))
                WOS[par].did_read("mm", gmm)
                OSS[par].did_read("mm", gmm)
            v_grp = S.val("mm")
            for dmc in range(4):
                p3_R[dmc].wrote("mm", v_grp)
            for dmc in range(4):
                O3.write("vector")
                p3_R[dmc].read("vector")
                ve1 = vop(lambda E, d=dmc: E.tensor_copy(o3_sb.ap(), p3_ps[d]))
                p3_R[dmc].did_read("vec", ve1)
                O3.wrote("vec", ve1)
                for tt in range(4):
                    tc = tokb * 4 + tt
                    PS_big.write("tensor")
                    O3.read("tensor")
                    tmm = tri(ps_big[:, tt * 128:tt * 128 + 128],
                              o3_sb.ap()[:, tt * 128:tt * 128 + 128], eye_f.ap(),
                              inc=True)
                    PS_big.wrote("mm", tmm)
                    O3.did_read("mm", tmm)
                    dmcol = dmg * 512 + dmc * 128
                    XQS[tt].read("vector")
                    PS_big.read("vector")
                    R1C[tc].write("vector")
                    vr1 = vop(lambda E, tt=tt, tc=tc, dmcol=dmcol: E.tensor_tensor(
                        r1.ap()[:, tc * D + dmcol:tc * D + dmcol + 128],
                        ps_big[:, tt * 128:tt * 128 + 128],
                        xqs.ap()[:, tt * D + dmcol:tt * D + dmcol + 128], ALU.add))
                    PS_big.did_read("vec", vr1)
                    XQS[tt].did_read("vec", vr1)
                    R1C[tc].wrote("vec", vr1)

        for tt in range(4):
            tc = tokb * 4 + tt
            rr = r1.ap()[:, tc * D:tc * D + D]
            if DEBUG_DUMP:
                S.wait("sync", "vec", R1C[tc].w[1])
                dma("sync", "std", r1_dbg[tc * 128:tc * 128 + 128, :], rr)
            vb = vop(lambda E, rr=rr: E.tensor_tensor(rr, rr, bo_bc.ap(), ALU.add))
            vop(lambda E, rr=rr: E.tensor_reduce(lnst.ap()[:, 0:1], rr, AX.X, ALU.add),
                inc=False)
            vdrain()
            vmean = vop(lambda E: E.tensor_scalar(
                lnst.ap()[:, 1:2], lnst.ap()[:, 0:1], -1.0 / D, None, ALU.mult))
            vdrain()
            S.wait("scalar", "vec", vmean)
            if DEBUG_DUMP:
                S.wait("sync", "vec", vmean)
                v_lnd0 = dma("sync", "std", ln_dbg[:, tc * 32:tc * 32 + 16], lnst.ap())
                S.wait("vector", "std", v_lnd0)
                S.wait("scalar", "std", v_lnd0)
            vsq = aop(lambda E, rr=rr: E.activation(
                sq_scr.ap(), rr, AF.Square, accum_out=lnst.ap()[:, 2:3]))
            S.wait("vector", "act", vsq)
            vop(lambda E: E.tensor_tensor(
                lnst.ap()[:, 3:4], lnst.ap()[:, 1:2], lnst.ap()[:, 1:2], ALU.mult),
                inc=False)
            vdrain()
            vvar = vop(lambda E: E.scalar_tensor_tensor(
                lnst.ap()[:, 4:5], lnst.ap()[:, 2:3], 1.0 / D, lnst.ap()[:, 3:4],
                ALU.mult, ALU.subtract))
            vdrain()
            S.wait("scalar", "vec", vvar)
            aop(lambda E: E.activation(lnst.ap()[:, 5:6], lnst.ap()[:, 4:5], AF.Ln,
                                       bias=consts.ap()[:, 0:1]), inc=False)
            vrstd = aop(lambda E: E.activation(lnst.ap()[:, 6:7], lnst.ap()[:, 5:6],
                                               AF.Exp, scale=-0.5))
            S.wait("vector", "act", vrstd)
            vop(lambda E: E.tensor_copy(spc.ap(), spc.ap()), inc=False)
            vdrain()
            vnmr = vop(lambda E: E.tensor_scalar(
                lnst.ap()[:, 7:8], lnst.ap()[:, 1:2], lnst.ap()[:, 6:7], None, ALU.mult))
            S.wait("scalar", "vec", vnmr)
            aop(lambda E: E.activation(spc.ap(), spc.ap(), AF.Identity), inc=False)
            X1SB.write("scalar")
            vx1 = aop(lambda E, rr=rr: E.activation(
                x1_sb.ap(), rr, AF.Identity,
                bias=lnst.ap()[:, 7:8], scale=lnst.ap()[:, 6:7]))
            X1SB.wrote("act", vx1)
            vx1 = aop(lambda E: E.activation(spc.ap(), spc.ap(), AF.Identity))
            X1SB.wrote("act", vx1)
            if DEBUG_DUMP:
                S.wait("sync", "act", vx1)
                v_lnd = dma("sync", "std", ln_dbg[:, tc * 32 + 16:tc * 32 + 32], lnst.ap())
                S.wait("vector", "std", v_lnd)
                S.wait("scalar", "std", v_lnd)
            X1SB.read("sync")
            v_x1st = dma("sync", "stx1", x1_scr[tc * 128:tc * 128 + 128, :], x1_sb.ap())
            X1SB.did_read("stx1", v_x1st)
            S.wait("vector", "act", vx1)
            vop(lambda E: E.tensor_reduce(
                lnst.ap()[:, 8:9], x1_sb.ap(), AX.X, ALU.max,
                apply_absolute_value=True), inc=False)
            vdrain()
            vaxc = vop(lambda E, tc=tc: E.tensor_scalar(
                sxa.ap()[:, tc:tc + 1], lnst.ap()[:, 8:9], 1e-5, None, ALU.max))
            vdrain()
            S.wait("scalar", "vec", vaxc)
            aop(lambda E, tc=tc: E.activation(
                lnst.ap()[:, 9:10], sxa.ap()[:, tc:tc + 1], AF.Ln), inc=False)
            vsx = aop(lambda E: E.activation(lnst.ap()[:, 10:11], lnst.ap()[:, 9:10],
                                             AF.Exp, scale=-1.0))
            S.wait("vector", "act", vsx)
            vop(lambda E: E.tensor_copy(qd.ap(), x1_sb.ap()[:, 0:512]), inc=False)
            vdrain()
            for cb in range(4):
                xsl = x1_sb.ap()[:, cb * 512:cb * 512 + 512]
                QKR.write("vector")
                vop(lambda E, xsl=xsl: E.tensor_scalar(
                    qy.ap(), xsl, lnst.ap()[:, 10:11], 127.0, ALU.mult, ALU.mult),
                    inc=False)
                vop(lambda E: E.tensor_copy(qi.ap(), qy.ap()), inc=False)
                vop(lambda E: E.tensor_tensor(qd.ap(), qy.ap(), qi.ap(), ALU.subtract),
                    inc=False)
                vop(lambda E: E.scalar_tensor_tensor(
                    qa.ap(), qd.ap(), 0.5, qi.ap(), ALU.is_gt, ALU.add), inc=False)
                vop(lambda E: E.tensor_scalar(qb.ap(), qd.ap(), -0.5, None, ALU.is_lt),
                    inc=False)
                vop(lambda E: E.tensor_tensor(qa.ap(), qa.ap(), qb.ap(), ALU.subtract),
                    inc=False)
                vqk = vop(lambda E: E.tensor_scalar(
                    qy.ap(), qa.ap(), -128.0, 127.0, ALU.max, ALU.min))
                QKR.wrote("vec", vqk)
                for s2 in range(4):
                    kcg = cb * 4 + s2
                    PS_a.write("tensor")
                    QKR.read("tensor")
                    tmm = tri(ps_a[:, 0:128], qy.ap()[:, s2 * 128:s2 * 128 + 128],
                              eye_f.ap(), inc=True)
                    PS_a.wrote("mm", tmm)
                    QKR.did_read("mm", tmm)
                    PS_a.read("vector")
                    XQFM.write("vector")
                    vxq = vop(lambda E, kcg=kcg, tc=tc: E.tensor_copy(
                        xq_fm.ap()[:, kcg * TQ + tc * 128:kcg * TQ + tc * 128 + 128],
                        ps_a[:, 0:128]))
                    PS_a.did_read("vec", vxq)
                    XQFM.wrote("vec", vxq)

    if STOP_AFTER == "p3":
        return early_out()
    if STOP_AFTER == "p3pad":
        for _ in range(200):
            vop(lambda E: E.nop(), inc=False)
        return early_out()
    for tc in range(8):
        vop(lambda E, tc=tc: E.tensor_scalar(
            dq1.ap()[:, tc:tc + 1], sxa.ap()[:, tc:tc + 1], wmean.ap()[:, 0:1],
            -1.0 / 127.0, ALU.mult, ALU.mult), inc=False)
        vop(lambda E, tc=tc: E.tensor_scalar(
            dq2.ap()[:, tc:tc + 1], sxa.ap()[:, tc:tc + 1], wmean.ap()[:, 1:2],
            -1.0 / 127.0, ALU.mult, ALU.mult), inc=False)
    v_dq = vop(lambda E: E.nop(nofuse=True))
    if STOP_AFTER == "p3dq":
        return early_out()
    if STOP_AFTER == "bar_t2":
        s, v = S.inc("mm")
        S.emit("tensor", lambda E, s=s: E.nop(nofuse=True).then_inc(s, 1))
        return early_out()
    if STOP_AFTER == "bar_t3":
        for e2 in ("sync", "vector", "scalar", "gpsimd"):
            S.wait(e2, "mm", S.val("mm"))
        return early_out()
    if STOP_AFTER in ("bar_t", "bar_v", "bar_s"):
        eng, sem = {"bar_t": ("tensor", "mm"), "bar_v": ("vector", "vec"),
                    "bar_s": ("scalar", "act")}[STOP_AFTER]
        s, v = S.inc(sem)
        S.emit(eng, lambda E, s=s: E.nop(nofuse=True).then_inc(s, 1))
        for e2 in ("sync", "tensor", "vector", "scalar", "gpsimd"):
            S.wait(e2, sem, v)
        return early_out()
    if STOP_AFTER == "p3bar_c":
        vs = {}
        for eng, sem in (("tensor", "mm"), ("vector", "vec"), ("scalar", "act")):
            s, v = S.inc(sem)
            S.emit(eng, lambda E, s=s: E.nop(nofuse=True).then_inc(s, 1))
            vs[sem] = v
        for eng in ("sync", "tensor", "vector", "scalar", "gpsimd"):
            for sem, v in vs.items():
                S.wait(eng, sem, v)
        return early_out()

    barrier()
    if DEBUG_DUMP:
        dma("sync", "std", xq_dbg, xq_fm.ap())
    if STOP_AFTER == "p5bar":
        return early_out()
    t1v = t1_scr.rearrange("(b p m) -> b p m", p=128, m=KC * 512)
    t2v = t2_scr.rearrange("(b p m) -> b p m", p=128, m=KC * 512)
    t3v = t3_scr.rearrange("(b p m) -> b p m", p=128, m=(HID // 128) * 128)

    TS = [Res(S), Res(S)]
    HQ1 = [Res(S), Res(S)]
    HQ2 = [Res(S), Res(S)]
    HQT = [Res(S), Res(S)]
    HQF = Res(S)
    T3S = [Res(S), Res(S)]
    R2C = [Res(S) for _ in range(4)]
    FRS = Res(S)
    YSB = Res(S)
    QKR2 = Res(S)

    p5_ps = {("g1", 0): (ps_a[:, 0:512], PS_a), ("g1", 1): (ps_mid[:, 0:512], PS_mid0),
             ("g2", 0): (ps_b[:, 0:512], PS_b), ("g2", 1): (ps_mid[:, 512:1024], PS_mid1)}
    p6_ps = [(ps_mid[:, 0:512], PS_mid0), (ps_mid[:, 512:1024], PS_mid1),
             (ps_big[:, 0:512], PS_big), (ps_big[:, 512:1024], PS_big)]
    p6_tr = [(ps_big[:, 1024 + i * 128:1024 + (i + 1) * 128], PS_big2) for i in range(4)]

    y_store = 0
    for half in range(2):
        if half == 1:
            barrier()
        for qq in range(2):
            tc0 = half * 4 + qq * 2
            for ob in range(11):
                if STOP_AFTER == "p5ob%d" % ob and half == 0 and qq == 0 and ob > 0:
                    return early_out()
                for m, tv, dqv in ((0, t1v, dq1), (1, t2v, dq2)):
                    par = (ob * 2 + m) % 2
                    TS[par].write("sync")
                    S.wait("sync", "gstt", bg_state["tern_done"][m])
                    v_t = dma("sync", "ldt%d" % par,
                              t_s.ap()[:, par * KC * 512:(par + 1) * KC * 512],
                              tv[ob])
                    TS[par].wrote("ldt%d" % par, v_t)
                    if STOP_AFTER == "p5a" and half == 0 and qq == 0 and ob == 0 and m == 0:
                        return early_out()
                    for tc in range(2):
                        tcg = tc0 + tc
                        key = ("g1" if m == 0 else "g2", tc)
                        psv, psR = p5_ps[key]
                        psR.write("tensor")
                        TS[par].read("tensor")
                        XQFM.read("tensor")
                        if half == 0 and qq == 0 and ob == 0 and m == 0 and tc == 0:
                            S.wait("tensor", "vec", v_dq)
                        for kc in range(KC):
                            lh = xq_fm.ap()[:, kc * TQ + tcg * 128:kc * TQ + tcg * 128 + 128]
                            rh = t_s.ap()[:, par * KC * 512 + kc * 512:
                                          par * KC * 512 + kc * 512 + 512]
                            last = kc == KC - 1
                            vmm = mmi(psv, lh, rh, kc == 0, last, inc=last)
                        psR.wrote("mm", vmm)
                        TS[par].did_read("mm", vmm)
                        XQFM.did_read("mm", vmm)
                        if STOP_AFTER == "p5b" and half == 0 and qq == 0 and ob == 0 and tc == 0:
                            return early_out()
                        if m == 0:
                            HQ1[tc].write("scalar")
                            psR.read("scalar")
                            va = aop(lambda E, pz=psv, tcg=tcg, tc=tc: E.activation(
                                hs1.ap()[:, tc * 512:tc * 512 + 512], pz, AF.Silu,
                                scale=dq1.ap()[:, tcg:tcg + 1]))
                            psR.did_read("act", va)
                            HQ1[tc].wrote("act", va)
                            if STOP_AFTER == "p5c" and half == 0 and qq == 0 and ob == 0 and tc == 0:
                                return early_out()
                        else:
                            HQ2[tc].write("vector")
                            psR.read("vector")
                            ve2 = vop(lambda E, pz=psv, tcg=tcg, tc=tc: E.tensor_scalar(
                                hs2.ap()[:, tc * 512:tc * 512 + 512], pz,
                                dq2.ap()[:, tcg:tcg + 1], None, ALU.mult))
                            psR.did_read("vec", ve2)
                            HQ2[tc].wrote("vec", ve2)
                            HQ1[tc].read("vector")
                            HQT[tc].write("vector")
                            vh = vop(lambda E, tc=tc, ob=ob: E.tensor_tensor(
                                h_q.ap()[:, tc * HID + ob * 512:tc * HID + ob * 512 + 512],
                                hs1.ap()[:, tc * 512:tc * 512 + 512],
                                hs2.ap()[:, tc * 512:tc * 512 + 512], ALU.mult))
                            HQ1[tc].did_read("vec", vh)
                            HQT[tc].wrote("vec", vh)
                            vdrain()
                            if ob == 0:
                                vop(lambda E, tcg=tcg, tc=tc, ob=ob: E.tensor_reduce(
                                    hamax.ap()[:, tcg:tcg + 1],
                                    h_q.ap()[:, tc * HID + ob * 512:
                                             tc * HID + ob * 512 + 512],
                                    AX.X, ALU.max, apply_absolute_value=True), inc=False)
                            else:
                                vop(lambda E, tc=tc, ob=ob: E.tensor_reduce(
                                    lnst.ap()[:, 12:13],
                                    h_q.ap()[:, tc * HID + ob * 512:
                                             tc * HID + ob * 512 + 512],
                                    AX.X, ALU.max, apply_absolute_value=True), inc=False)
                                vdrain()
                                vop(lambda E, tcg=tcg: E.tensor_tensor(
                                    hamax.ap()[:, tcg:tcg + 1], hamax.ap()[:, tcg:tcg + 1],
                                    lnst.ap()[:, 12:13], ALU.max), inc=False)
            if STOP_AFTER == "p5mm_h%d_q%d" % (half, qq):
                return early_out()
            if DEBUG_DUMP and half == 0 and qq == 0:
                S.wait("sync", "vec", S.val("vec"))
                v_hd = dma("sync", "std", h_dbg, h_q.ap())
                S.wait("vector", "std", v_hd)
            if DEBUG_DUMP and half == 1 and qq == 1:
                S.wait("sync", "vec", S.val("vec"))
                v_hd2 = dma("sync", "std", h2_dbg, h_q.ap())
                S.wait("vector", "std", v_hd2)
            for tc in range(2):
                tcg = tc0 + tc
                vdrain()
                vaxc = vop(lambda E, tcg=tcg: E.tensor_scalar(
                    lnst.ap()[:, 8:9], hamax.ap()[:, tcg:tcg + 1], 1e-5, None, ALU.max))
                vdrain()
                S.wait("scalar", "vec", vaxc)
                aop(lambda E: E.activation(lnst.ap()[:, 9:10], lnst.ap()[:, 8:9], AF.Ln),
                    inc=False)
                vsh = aop(lambda E: E.activation(lnst.ap()[:, 10:11], lnst.ap()[:, 9:10],
                                                 AF.Exp, scale=-1.0))
                S.wait("vector", "act", vsh)
                vop(lambda E, tcg=tcg: E.tensor_scalar(
                    dq3.ap()[:, tcg:tcg + 1], lnst.ap()[:, 8:9], wmean.ap()[:, 2:3],
                    -1.0 / 127.0, ALU.mult, ALU.mult), inc=False)
                vop(lambda E, tc=tc: E.tensor_copy(
                    qd.ap(), h_q.ap()[:, tc * HID:tc * HID + 512]), inc=False)
                vdrain()
                for cb in range(11):
                    hsl = h_q.ap()[:, tc * HID + cb * 512:tc * HID + cb * 512 + 512]
                    QKR2.write("vector")
                    vop(lambda E, hsl=hsl: E.tensor_scalar(
                        qy.ap(), hsl, lnst.ap()[:, 10:11], 127.0, ALU.mult, ALU.mult),
                        inc=False)
                    vop(lambda E: E.tensor_copy(qi.ap(), qy.ap()), inc=False)
                    vop(lambda E: E.tensor_tensor(qd.ap(), qy.ap(), qi.ap(),
                                                  ALU.subtract), inc=False)
                    vop(lambda E: E.scalar_tensor_tensor(
                        qa.ap(), qd.ap(), 0.5, qi.ap(), ALU.is_gt, ALU.add), inc=False)
                    for thr in (1.5, 2.5, 3.5):
                        vop(lambda E, thr=thr: E.scalar_tensor_tensor(
                            qa.ap(), qd.ap(), thr, qa.ap(), ALU.is_gt, ALU.add),
                            inc=False)
                    vop(lambda E: E.tensor_scalar(qb.ap(), qd.ap(), -0.5, None,
                                                  ALU.is_lt), inc=False)
                    for thr in (-1.5, -2.5, -3.5):
                        vop(lambda E, thr=thr: E.scalar_tensor_tensor(
                            qb.ap(), qd.ap(), thr, qb.ap(), ALU.is_lt, ALU.add),
                            inc=False)
                    vop(lambda E: E.tensor_tensor(qa.ap(), qa.ap(), qb.ap(),
                                                  ALU.subtract), inc=False)
                    vqk = vop(lambda E: E.tensor_scalar(
                        qy.ap(), qa.ap(), -128.0, 127.0, ALU.max, ALU.min))
                    QKR2.wrote("vec", vqk)
                    if DEBUG_DUMP and half == 1 and qq == 0 and tc == 1 and cb == 3:
                        S.wait("sync", "vec", vqk)
                        dma("sync", "std", q_dbg[:, 0:512], qd.ap())
                        dma("sync", "std", q_dbg[:, 512:1024], qa.ap())
                        dma("sync", "std", q_dbg[:, 1024:1536], qy.ap())
                        v_qd2 = dma("sync", "std", q_dbg[:, 1536:2048], qb.ap())
                        S.wait("vector", "std", v_qd2)
                    for s2 in range(4):
                        ocg = cb * 4 + s2
                        PS_a.write("tensor")
                        QKR2.read("tensor")
                        tmm = tri(ps_a[:, 0:128], qy.ap()[:, s2 * 128:s2 * 128 + 128],
                                  eye_f.ap(), inc=True)
                        PS_a.wrote("mm", tmm)
                        QKR2.did_read("mm", tmm)
                        PS_a.read("vector")
                        HQF.write("vector")
                        col = ocg * 512 + (qq * 2 + tc) * 128
                        vhq = vop(lambda E, col=col: E.tensor_copy(
                            hq_fm.ap()[:, col:col + 128], ps_a[:, 0:128]))
                        PS_a.did_read("vec", vhq)
                        HQF.wrote("vec", vhq)
                HQT[tc].did_read("vec", S.val("vec"))
            if STOP_AFTER == "p5qq%d_h%d" % (qq, half):
                return early_out()

        if STOP_AFTER == "p5h%d" % half:
            return early_out()
        if DEBUG_DUMP:
            S.wait("sync", "vec", S.val("vec"))
            if half == 1:
                dma("sync", "std", ha_dbg, hamax.ap())
        if DEBUG_DUMP and half == 1:
            S.wait("sync", "vec", S.val("vec"))
            v_hqd = dma("sync", "std", hq_dbg, hq_fm.ap())
            S.wait("vector", "std", v_hqd)
            S.wait("tensor", "std", v_hqd)
        for tcz in range(4):
            R2C[tcz].write("sync")
        for tc in range(2):
            HQT[tc].write("sync")
        S.wait("sync", "stx1", S.val("stx1"))
        for cz in range(4):
            tcg0 = (half * 4 + cz) * 128
            v_r2 = dma("sync", "ldr2", r2.ap()[:, cz * D:(cz + 1) * D],
                       x1_scr[tcg0:tcg0 + 128, :])
        for tcz in range(4):
            R2C[tcz].wrote("ldr2", v_r2)
        for dmc in range(KC):
            par = dmc % 2
            T3S[par].write("sync")
            S.wait("sync", "gstt", bg_state["tern_done"][2])
            v_t3 = dma("sync", "ldt3%d" % par,
                       t3_sb.ap()[:, par * HID:(par + 1) * HID], t3v[dmc])
            T3S[par].wrote("ldt3%d" % par, v_t3)
            pslot = dmc % 4
            psv, psR = p6_ps[pslot]
            psR.write("tensor")
            T3S[par].read("tensor")
            HQF.read("tensor")
            for oc in range(HID // 128):
                lh = t3_sb.ap()[:, par * HID + oc * 128:par * HID + oc * 128 + 128]
                rh = hq_fm.ap()[:, oc * 512:oc * 512 + 512]
                last = oc == HID // 128 - 1
                vmm = mmi(psv, lh, rh, oc == 0, last, inc=last)
            psR.wrote("mm", vmm)
            T3S[par].did_read("mm", vmm)
            HQF.did_read("mm", vmm)
            FRS.write("vector")
            psR.read("vector")
            vfr = vop(lambda E, pz=psv: E.tensor_copy(fr_sb.ap()[:, 0:512], pz))
            psR.did_read("vec", vfr)
            FRS.wrote("vec", vfr)
            for tt in range(4):
                tcg = half * 4 + tt
                trv, trR = p6_tr[tt]
                trR.write("tensor")
                FRS.read("tensor")
                tmm = tri(trv, fr_sb.ap()[:, tt * 128:tt * 128 + 128], eye_f.ap(),
                          inc=True)
                trR.wrote("mm", tmm)
                FRS.did_read("mm", tmm)
                trR.read("vector")
                R2C[tt].read("vector")
                R2C[tt].write("vector")
                vr2 = vop(lambda E, tt=tt, tcg=tcg, dmc=dmc, trv=trv:
                          E.scalar_tensor_tensor(
                              r2.ap()[:, tt * D + dmc * 128:tt * D + dmc * 128 + 128],
                              trv, dq3.ap()[:, tcg:tcg + 1],
                              r2.ap()[:, tt * D + dmc * 128:tt * D + dmc * 128 + 128],
                              ALU.mult, ALU.add))
                trR.did_read("vec", vr2)
                R2C[tt].wrote("vec", vr2)

        if DEBUG_DUMP:
            S.wait("sync", "vec", S.val("vec"))
            for tz in range(4):
                v_r2d = dma("sync", "std", r2_dbg[(half * 4 + tz) * 128:(half * 4 + tz) * 128 + 128, :],
                            r2.ap()[:, tz * D:(tz + 1) * D])
            S.wait("vector", "std", v_r2d)
        if STOP_AFTER == "p6h%d" % half:
            return early_out()
        for tt in range(4):
            tcg = half * 4 + tt
            rr = r2.ap()[:, tt * D:tt * D + D]
            S.wait("vector", "vec", R2C[tt].w[1] if R2C[tt].w else 0)
            vop(lambda E, rr=rr: E.tensor_reduce(lnst.ap()[:, 0:1], rr, AX.X, ALU.add),
                inc=False)
            vdrain()
            vmean = vop(lambda E: E.tensor_scalar(
                lnst.ap()[:, 1:2], lnst.ap()[:, 0:1], -1.0 / D, None, ALU.mult))
            vdrain()
            S.wait("scalar", "vec", vmean)
            R2C[tt].read("scalar")
            vsq = aop(lambda E, rr=rr: E.activation(
                sq_scr.ap(), rr, AF.Square, accum_out=lnst.ap()[:, 2:3]))
            R2C[tt].did_read("act", vsq)
            S.wait("vector", "act", vsq)
            vop(lambda E: E.tensor_tensor(
                lnst.ap()[:, 3:4], lnst.ap()[:, 1:2], lnst.ap()[:, 1:2], ALU.mult),
                inc=False)
            vdrain()
            vvar = vop(lambda E: E.scalar_tensor_tensor(
                lnst.ap()[:, 4:5], lnst.ap()[:, 2:3], 1.0 / D, lnst.ap()[:, 3:4],
                ALU.mult, ALU.subtract))
            vdrain()
            S.wait("scalar", "vec", vvar)
            aop(lambda E: E.activation(lnst.ap()[:, 5:6], lnst.ap()[:, 4:5], AF.Ln,
                                       bias=consts.ap()[:, 0:1]), inc=False)
            vrstd = aop(lambda E: E.activation(lnst.ap()[:, 6:7], lnst.ap()[:, 5:6],
                                               AF.Exp, scale=-0.5))
            S.wait("vector", "act", vrstd)
            vop(lambda E: E.tensor_copy(spc.ap(), spc.ap()), inc=False)
            vdrain()
            vnmr = vop(lambda E: E.tensor_scalar(
                lnst.ap()[:, 7:8], lnst.ap()[:, 1:2], lnst.ap()[:, 6:7], None, ALU.mult))
            S.wait("scalar", "vec", vnmr)
            aop(lambda E: E.activation(spc.ap(), spc.ap(), AF.Identity), inc=False)
            YSB.write("scalar")
            vy = aop(lambda E, rr=rr: E.activation(
                y_sb.ap(), rr, AF.Identity,
                bias=lnst.ap()[:, 7:8], scale=lnst.ap()[:, 6:7]))
            YSB.wrote("act", vy)
            vy = aop(lambda E: E.activation(spc.ap(), spc.ap(), AF.Identity))
            YSB.wrote("act", vy)
            YSB.read("sync")
            y_store = dma("sync", "sty", y_out[tcg * 128:tcg * 128 + 128, :], y_sb.ap())
            YSB.did_read("sty", y_store)

    for sem in ("sty", "stx1", "gstt", "gso0", "gso1"):
        S.wait("sync", sem, S.val(sem))

    _emit_block()
    return nc


def kernel(L0_attn_x, L0_attn_Wq, L0_attn_bq, L0_attn_Wk, L0_attn_bk,
           L0_attn_Wv, L0_attn_bv, L0_attn_Wo, L0_attn_bo,
           L0_ffn_w1, L0_ffn_w2, L0_ffn_w3):
    x = np.ascontiguousarray(np.asarray(L0_attn_x, dtype=np.float32))
    Wq = np.asarray(L0_attn_Wq, dtype=np.float32)
    Wk = np.asarray(L0_attn_Wk, dtype=np.float32)
    Wv = np.asarray(L0_attn_Wv, dtype=np.float32)
    Wo = np.ascontiguousarray(np.asarray(L0_attn_Wo, dtype=np.float32))
    bq = np.ascontiguousarray(np.asarray(L0_attn_bq, dtype=np.float32).reshape(16, 128).T)
    bk = np.ascontiguousarray(np.asarray(L0_attn_bk, dtype=np.float32).reshape(16, 128).T)
    bv = np.asarray(L0_attn_bv, dtype=np.float32).reshape(1, D).copy()
    bo = np.asarray(L0_attn_bo, dtype=np.float32).reshape(1, D).copy()
    w1 = np.asarray(L0_ffn_w1, dtype=np.float32)
    w2 = np.asarray(L0_ffn_w2, dtype=np.float32)
    w3 = np.asarray(L0_ffn_w3, dtype=np.float32)

    def blk_w_qkv(W):
        return np.ascontiguousarray(
            W.reshape(KC, 128, NHG, 256).transpose(2, 1, 0, 3).reshape(NHG, 128, KC * 256))

    def blk_ffn(wm):
        return np.ascontiguousarray(
            wm.reshape(11, 512, KC, 128).transpose(0, 3, 2, 1).reshape(11, 128, KC * 512))

    def blk_w3(wm):
        return np.ascontiguousarray(
            wm.reshape(KC, 128, HID // 128, 128).transpose(0, 3, 2, 1)
            .reshape(KC, 128, HID))

    Wkb, Wvb = blk_w_qkv(Wk), blk_w_qkv(Wv)
    Wqb2 = np.ascontiguousarray(
        Wq.reshape(KC, 128, KC, 128).transpose(2, 1, 0, 3).reshape(KC, 128, D))
    w1b, w2b, w3b = blk_ffn(w1), blk_ffn(w2), blk_w3(w3)
    eyeb = np.eye(128, dtype=ml_dtypes.bfloat16)
    eyef = np.eye(128, dtype=np.float32)

    masks = []
    qb_ = np.arange(8)[:, None, None]
    a_ = np.arange(8)[None, None, :]
    b_ = np.arange(16)[None, :, None]
    tt_local = (qb_ * 128 + a_ * 16 + b_).reshape(TQ)
    kidx = np.arange(T)[None, :]
    for half in range(2):
        qidx = (tt_local + half * TQ)[:, None]
        m = np.where(kidx <= qidx, np.float32(0.0), np.float32(-1.0e30))
        masks.append(m.astype(ml_dtypes.bfloat16))
    eyep = np.zeros((128, 128), np.float32)
    for b2 in range(16):
        for a2 in range(8):
            eyep[b2 * 8 + a2, a2 * 16 + b2] = 1.0

    nc = build_program()

    def blk_x(xfm):
        nb = xfm.shape[1] // 256
        return np.ascontiguousarray(
            xfm.reshape(KC, 128, nb, 256).transpose(2, 1, 0, 3).reshape(nb, 128, KC * 256))

    in_maps = []
    for c in range(8):
        b, half = c // 2, c % 2
        xb = x[b]
        xq = xb[half * TQ:(half + 1) * TQ]
        hh_i = np.arange(16)[:, None]
        u_i = np.arange(64)[None, :]
        scatter = (hh_i * 128 + 64 * half + u_i).reshape(-1)
        xqs_cols = np.ascontiguousarray(xb[scatter].T)
        in_maps.append(dict(
            x_kvb=blk_x(xb.T),
            x_qb=blk_x(xqs_cols),
            x_q_tm=np.ascontiguousarray(xq),
            Wqb2=Wqb2, Wkb=Wkb, Wvb=Wvb, Wo=Wo,
            bq=bq, bk=bk, bv=bv, bo=bo,
            w1b=w1b, w2b=w2b, w3b=w3b,
            mask=masks[half], eyeb=eyeb, eyef=eyef, eyep=eyep,
        ))
    res = run_bass_kernel_spmd(nc, in_maps, core_ids=list(range(8)))
    y = np.empty((4, T, D), dtype=np.float32)
    for c in range(8):
        b, half = c // 2, c % 2
        y[b, half * TQ:(half + 1) * TQ, :] = res.results[c]["y"]
    return y
